# version 30
# baseline (speedup 1.0000x reference)
from contextlib import ExitStack
import math
import numpy as np
import ml_dtypes
import concourse.bass as bass
import concourse.mybir as mybir
from concourse.bass_utils import run_bass_kernel_spmd

F32 = mybir.dt.float32
BF16 = mybir.dt.bfloat16
I32 = mybir.dt.int32
AF = mybir.ActivationFunctionType
ALU = mybir.AluOpType
AX = mybir.AxisListType

S = 4096
D = 2048
NH = 16
OWN = 1024
NT = S // 128
NTO = OWN // 128
FQKV = 6144
LAT = 512
IEXP = 1024
NRE = 7
EPS = 1e-5
ISC = 1.0 / math.sqrt(128.0)
ROPE_W = 1536
MROW = 5632
MOFF = 4608
V129 = NH * 129

ENGS = ("gp", "sy", "te", "ve", "se")


class Sem:
    __slots__ = ("h", "n", "name", "is_dma", "queue")

    def __init__(self, h, name, is_dma=False, queue=None):
        self.h = h
        self.n = 0
        self.name = name
        self.is_dma = is_dma
        self.queue = queue


class MK:
    def __init__(self, nc, es, taps=()):
        self.nc = nc
        self.es = es
        self.taps = set(taps)
        self.tap_specs = []
        self.plan = {k: [] for k in ENGS}
        self.wmax = {k: {} for k in ENGS}
        self.streams = {}
        self.gld = self.dsem("gp", "g0")
        self.sld = self.dsem("sy", "s0")
        self.esem = {k: self.mksem(k + "m") for k in ENGS}
        self.bar = self.mksem("bar")

    def mksem(self, name):
        return Sem(self.es.enter_context(self.nc.semaphore(name)), name)

    def dsem(self, q, name):
        key = (q, name)
        if key not in self.streams:
            self.streams[key] = Sem(
                self.es.enter_context(self.nc.semaphore(f"D{q}_{name}")),
                f"D{q}_{name}", is_dma=True, queue=q)
        return self.streams[key]

    def sbuf(self, name, shape, dtype, **kw):
        return self.es.enter_context(self.nc.sbuf_tensor(name, shape, dtype, **kw))

    def emit(self, eng, fn):
        self.plan[eng].append(fn)

    def wait(self, eng, tok):
        if tok is None:
            return
        sem, v = tok
        if sem.is_dma:
            v = sem.n
        if v <= 0 or self.wmax[eng].get(sem, 0) >= v:
            return
        self.wmax[eng][sem] = v
        self.emit(eng, lambda e, s=sem, v=v: e.wait_ge(s.h, v))

    def dma(self, q, out, in_, waits=(), stream=None, **kw):
        for t in waits:
            self.wait(q, t)
        s = self.dsem(q, stream) if stream else (
            self.gld if q == "gp" else self.sld)
        s.n += 16
        v = s.n
        self.emit(q, lambda e, o=out, i=in_, s=s, kw=kw:
                  e.dma_start(out=o, in_=i, **kw).then_inc(s.h, 16))
        return (s, v)

    def dmat(self, out, in_, waits=(), stream=None, q="sy"):
        for t in waits:
            self.wait(q, t)
        s = self.dsem(q, stream) if stream else self.sld
        s.n += 16
        v = s.n
        self.emit(q, lambda e, o=out, i=in_, s=s:
                  e.dma_start_transpose(out=o, in_=i).then_inc(s.h, 16))
        return (s, v)

    def op(self, eng, meth, *args, waits=(), inc=False, **kw):
        for t in waits:
            self.wait(eng, t)
        tok = None
        if inc:
            sm = self.esem[eng]
            sm.n += 1
            tok = (sm, sm.n)

        def fn(e, meth=meth, args=args, kw=kw, tok=tok):
            ins = getattr(e, meth)(*args, **kw)
            if tok is not None:
                ins.then_inc(tok[0].h, 1)
        self.emit(eng, fn)
        return tok

    def drain(self, eng):
        self.emit(eng, lambda e: e.drain())

    def barrier(self):
        for (q, name), s in self.streams.items():
            if name in ("wcq", "wcm"):
                continue
            self.wait(q, (s, s.n))
        b = self.bar
        target = b.n + 5
        b.n = target
        for eng in ENGS:
            if eng in ("ve", "se", "gp", "te"):
                self.drain(eng)
            self.emit(eng, lambda e, b=b: e.sem_inc(b.h, 1))
        for eng in ENGS:
            self.wait(eng, (b, target))

    def tap(self, name, src_ap, shape, dtype):
        if name in self.taps:
            o = self.nc.dram_tensor("tap_" + name, shape, dtype,
                                    kind="ExternalOutput").ap()
            self.tap_specs.append((o, src_ap))

    def flush_taps(self):
        self.barrier()
        for o, src in self.tap_specs:
            self.dma("sy", o, src, stream="tapo")
        self.wait("sy", (self.sld, self.sld.n))
        self.wait("gp", (self.gld, self.gld.n))

    def flush_block(self):
        if not any(self.plan.values()):
            return
        with self.nc.Block() as block:
            for eng, setter in (("gp", block.gpsimd), ("sy", block.sync),
                                ("te", block.tensor), ("ve", block.vector),
                                ("se", block.scalar)):
                if not self.plan[eng]:
                    continue

                def runner(e, fns=tuple(self.plan[eng])):
                    for fn in fns:
                        fn(e)
                setter(runner)
        self.plan = {k: [] for k in ENGS}


def rearr(ap, spec, **kw):
    return ap.rearrange(spec, **kw)


class Build:
    def __init__(self, nc, mk, es):
        self.nc = nc
        self.mk = mk
        self.es = es

    def sb(self, st, n, sh, dt):
        return st.enter_context(self.nc.sbuf_tensor(n, sh, dt))

    def pt(self, st, n, sh):
        return st.enter_context(self.nc.psum_tensor(n, sh, F32))


def build(nc, taps=(), phases=99):
    es = ExitStack()
    with es:
        mk = MK(nc, es, taps)
        _body(nc, mk, es, phases)
        mk.flush_block()
    return nc


def _body(nc, mk, es, phases):
    B = Build(nc, mk, es)
    inp = {}

    def ein(name, shape):
        inp[name] = nc.dram_tensor(name, shape, F32, kind="ExternalInput").ap()

    ein("xb", [S, D]); ein("xbT", [D, S])
    ein("wqkvT", [D, FQKV]); ein("woutT", [D, D])
    ein("ln1", [1, D]); ein("ln2", [1, D])
    ein("kvdw", [D, LAT]); ein("qdw", [D, LAT])
    ein("kuw", [LAT, 1024]); ein("vuw", [LAT, D]); ein("quw", [LAT, 1024])
    ein("ropekw", [D, 1024]); ein("ropeqw", [LAT, 1024])
    ein("ow", [D, D])
    ein("shg", [D, IEXP]); ein("shu", [D, IEXP]); ein("shd", [IEXP, D])
    ein("rg", [NRE, D, IEXP]); ein("ru", [NRE, D, IEXP]); ein("rd", [NRE, IEXP, D])
    ein("rw", [D, NRE]); ein("rbias", [1, NRE])
    ein("qstart", [1, 1]); ein("osel", [1, 4])
    out = nc.dram_tensor("out", [OWN, D], F32, kind="ExternalOutput").ap()

    dram = lambda n, sh, dt: nc.dram_tensor(n, sh, dt, kind="Internal").ap()
    Qo = dram("Qo", [S, D], BF16)
    Ko = dram("Ko", [S, D], BF16)
    Vo = dram("Vo", [S, V129], BF16)
    O1 = dram("O1", [S, D], BF16)
    X1 = dram("X1", [S, D], F32)
    Zd = dram("Zd", [S, D], BF16)
    KVD = dram("KVD", [S, LAT], BF16)
    K2 = dram("K2", [S, D], BF16)
    V2 = dram("V2", [S, V129], BF16)
    ZOWN = dram("ZOWN", [OWN, D], BF16)
    XO = dram("XO", [OWN, D], F32)
    QD = dram("QD", [OWN, LAT], BF16)
    Q2 = dram("Q2", [OWN, D], BF16)
    O2d = dram("O2d", [OWN, D], BF16)
    Z2 = dram("Z2", [OWN, D], BF16)
    SP = dram("SP", [OWN, D], F32)
    ROPET = dram("ROPET", [S, ROPE_W], BF16)
    WTD = dram("WTD", [8, 1024], BF16)
    WQB = dram("WQB", [D, FQKV], BF16)
    WGB = dram("WGB", [8, D, IEXP], BF16)
    WUB = dram("WUB", [8, D, IEXP], BF16)
    WDB = dram("WDB", [8, IEXP, D], BF16)
    NRMD = dram("NRMD", [64, 128], F32)

    ones_row = mk.sbuf("ones_row", [1, 128], F32)
    ones_rowb = mk.sbuf("ones_rowb", [1, 128], BF16)
    ones_colb = mk.sbuf("ones_colb", [128, 1], BF16)
    one_one = mk.sbuf("one_one", [1, 1], F32)
    iota_col = mk.sbuf("iota_col", [128, 1], F32)
    ident = mk.sbuf("ident", [128, 128], BF16)
    ln1c = mk.sbuf("ln1c", [128, 16], F32)
    ln2c = mk.sbuf("ln2c", [128, 16], F32)
    biasO = mk.sbuf("biasO", [128, 16], F32)
    bias2 = mk.sbuf("bias2", [128, 16], F32)
    qstart_c = mk.sbuf("qstart_c", [128, 1], F32)
    oselc = mk.sbuf("oselc", [128, 4], F32)
    junkb = mk.sbuf("junkb", [128, 2048], BF16)
    eps_col = mk.sbuf("eps_col", [128, 1], F32)
    nk2g = mk.sbuf("nk2g", [128, 16], F32)

    ctx = dict(locals())

    p0_consts(B, ctx)
    mk.tap("ropet", ROPET[0:S, :], [S, ROPE_W], BF16)
    if phases >= 1:
        p1_qkv(B, ctx)
        mk.tap("Qo", Qo[0:S, :], [S, D], BF16)
        mk.tap("Ko", Ko[0:S, :], [S, D], BF16)
        mk.tap("Vo", Vo[0:S, :], [S, V129], BF16)
        mk.tap("biasO", biasO[:, :], [128, 16], F32)
    if phases >= 2:
        flash(B, ctx, outer=True)
        mk.tap("O1", O1[0:S, :], [S, D], BF16)
    if phases >= 3:
        p3_outproj(B, ctx)
        mk.tap("X1", X1[0:S, :], [S, D], F32)
        mk.tap("Zd", Zd[0:S, :], [S, D], BF16)
    if phases >= 4:
        p5_mlakv(B, ctx)
        mk.tap("K2", K2[0:S, :], [S, D], BF16)
        mk.tap("V2", V2[0:S, :], [S, V129], BF16)
        mk.tap("KVD", KVD[0:S, :], [S, LAT], BF16)
    if phases >= 5:
        p6_gather(B, ctx)
        p7_qside(B, ctx)
        mk.tap("Q2", Q2[0:OWN, :], [OWN, D], BF16)
        mk.tap("bias2", bias2[:, :], [128, 16], F32)
        mk.tap("ZOWN", ZOWN[0:OWN, :], [OWN, D], BF16)
    if phases >= 6:
        flash(B, ctx, outer=False)
        mk.tap("O2d", O2d[0:OWN, :], [OWN, D], BF16)
    if phases >= 7:
        p9_ow(B, ctx)
        mk.tap("Z2", Z2[0:OWN, :], [OWN, D], BF16)
        mk.tap("SP", SP[0:OWN, :], [OWN, D], F32)
    if phases >= 8:
        p10_moe(B, ctx)
        mk.flush_taps()
        return
    mk.flush_taps()
    dummy_out(mk, out)


def dummy_out(mk, out):
    z = mk.sbuf("zz_dummy", [128, D], F32)
    t = mk.op("ve", "memset", z[:, :], 0.0, inc=True)
    last = None
    for i in range(OWN // 128):
        last = mk.dma("sy", out[i * 128:(i + 1) * 128, :], z[:, :], waits=(t,), stream="tapo")
    mk.wait("sy", last)
    mk.wait("gp", (mk.gld, mk.gld.n))


def p0_consts(B, c):
    nc, mk = B.nc, B.mk
    inp = c["inp"]
    with ExitStack() as st:
        sb = lambda n, sh, dt: B.sb(st, n, sh, dt)
        WQB, WGB, WUB, WDB = c["WQB"], c["WGB"], c["WUB"], c["WDB"]
        for piece in range(4):
            sl = slice(piece * (FQKV // 4), (piece + 1) * (FQKV // 4))
            mk.dma("gp", WQB[:, sl], inp["wqkvT"][:, sl], stream="wcq")
        for e in range(8):
            mk.dma("gp", WGB[e], inp["rg"][e] if e < NRE else inp["shg"],
                   stream="wcm")
            mk.dma("gp", WUB[e], inp["ru"][e] if e < NRE else inp["shu"],
                   stream="wcm")
            mk.dma("gp", WDB[e], inp["rd"][e] if e < NRE else inp["shd"],
                   stream="wcm")
        c["wcq_tok"] = (mk.streams[("gp", "wcq")], mk.streams[("gp", "wcq")].n)
        c["wcm_tok"] = (mk.streams[("gp", "wcm")], mk.streams[("gp", "wcm")].n)
        mk.op("gp", "memset", c["ones_row"][:, :], 1.0)
        mk.op("gp", "memset", c["ones_rowb"][:, :], 1.0)
        mk.op("gp", "memset", c["ones_colb"][:, :], 1.0)
        mk.op("gp", "memset", c["one_one"][:, :], 1.0)
        mk.op("gp", "memset", c["eps_col"][:, :], EPS)
        g1 = mk.op("gp", "iota", c["iota_col"][:, :], [[1, 1]],
                   channel_multiplier=1, allow_small_or_imprecise_dtypes=True,
                   inc=True)
        irow = sb("irow", [128, 128], F32)
        g2 = mk.op("gp", "iota", irow[:, :], [[1, 128]], channel_multiplier=0,
                   allow_small_or_imprecise_dtypes=True, inc=True)
        v = mk.op("ve", "tensor_scalar", irow[:, :], irow[:, :],
                  c["iota_col"][:, :], None, ALU.subtract, waits=(g1, g2),
                  inc=True)
        mk.drain("ve")
        mk.op("ve", "tensor_single_scalar", c["ident"][:, :], irow[:, :],
              0.0, ALU.is_equal, inc=True)

        qs_sb = sb("qs_sb", [1, 1], F32)
        os_sb = sb("os_sb", [1, 4], F32)
        psb = B.pt(st, "psb", [128, 1])
        tq = mk.dma("gp", qs_sb[:, :], inp["qstart"], stream="p0ld")
        to = mk.dma("gp", os_sb[:, :], inp["osel"], stream="p0ld")
        t1 = mk.op("te", "matmul", psb[:, :], c["ones_row"][:, :], qs_sb[:, :],
                   start=True, stop=True, waits=(tq,), inc=True)
        pv = mk.op("ve", "tensor_copy", c["qstart_c"][:, :], psb[:, :],
                   waits=(t1,), inc=True)
        for q in range(4):
            t1 = mk.op("te", "matmul", psb[:, :], c["ones_row"][:, :],
                       os_sb[0:1, q:q + 1], start=True, stop=True,
                       waits=(to, pv), inc=True)
            pv = mk.op("ve", "tensor_copy", c["oselc"][:, q:q + 1], psb[:, :],
                       waits=(t1,), inc=True)
        lnrow = sb("lnrow", [1, D], F32)
        for name, dst in (("ln1", c["ln1c"]), ("ln2", c["ln2c"])):
            t = mk.dma("gp", lnrow[:, :], inp[name], waits=(pv,), stream="p0ld")
            for ec in range(16):
                t1 = mk.op("te", "matmul", psb[:, :],
                           lnrow[0:1, ec * 128:(ec + 1) * 128], c["one_one"][:, :],
                           start=True, stop=True, waits=(t, pv), inc=True)
                pv = mk.op("ve", "tensor_copy", dst[:, ec:ec + 1], psb[:, :],
                           waits=(t1,), inc=True)

        invf = sb("invf", [1, 32], F32)
        coefrow = sb("coefrow", [1, ROPE_W], F32)
        shiftrow = sb("shiftrow", [1, ROPE_W], F32)
        g3 = mk.op("gp", "iota", invf[:, :], [[1, 32]], channel_multiplier=0,
                   allow_small_or_imprecise_dtypes=True, inc=True)
        stt = mk.op("se", "activation", invf[:, :], invf[:, :], AF.Exp,
                    scale=-math.log(10000.0) / 32.0, waits=(g3,), inc=True)
        cm = 1.0 / (2.0 * math.pi)
        cr4 = rearr(coefrow.ap()[0:1, 0:512], "o (a i t) -> o a i t", a=8, t=2)
        iv4 = rearr(invf.ap(), "o (a i t) -> o a i t", a=1, t=1)
        v = None
        for a in range(8):
            for tp in range(2):
                v = mk.op("ve", "tensor_scalar", cr4[:, a:a + 1, :, tp:tp + 1],
                          iv4, cm, None, ALU.mult, waits=(stt,), inc=True)
        cr8 = rearr(coefrow.ap()[0:1, 512:1536], "o (a j) -> o a j", j=32)
        iv3 = rearr(invf.ap(), "o (a j) -> o a j", a=1)
        for blk in range(32):
            v = mk.op("ve", "tensor_scalar", cr8[:, blk:blk + 1, :], iv3, cm,
                      None, ALU.mult, inc=True)
        v = mk.op("ve", "memset", shiftrow[:, :], 0.0, inc=True)
        mk.drain("ve")
        v = mk.op("ve", "memset", shiftrow[0:1, 0:256], 0.25, inc=True)
        sh4 = rearr(shiftrow.ap()[0:1, 256:512], "o (i t) -> o i t", t=2)
        v = mk.op("ve", "memset", sh4[:, :, 0:1], 0.5, inc=True)
        v = mk.op("ve", "memset", shiftrow[0:1, 512:1024], 0.25, inc=True)
        sh8 = rearr(shiftrow.ap()[0:1, 1024:1536], "o (a j) -> o a j", j=32)
        for blk in range(0, 16, 2):
            v = mk.op("ve", "memset", sh8[:, blk:blk + 1, :], 0.5, inc=True)
        coef_tok = v
        mk.drain("ve")

        tposr = sb("tposr", [1, 128], F32)
        angi = sb("angi", [128, ROPE_W], I32)
        angf = sb("angf", [128, ROPE_W], F32)
        angc = sb("angc", [128, ROPE_W], F32)
        rtab = sb("rtab", [128, ROPE_W], BF16)
        psa = st.enter_context(nc.psum_tensor("psa", [128, ROPE_W], F32))
        g4 = mk.op("gp", "iota", tposr[:, :], [[1, 128]], channel_multiplier=0,
                   allow_small_or_imprecise_dtypes=True, inc=True)
        prev_sin = None
        prev_store = None
        ROPET = c["ROPET"]
        for tt in range(NT):
            if tt:
                v = mk.op("ve", "tensor_single_scalar", tposr[:, :], tposr[:, :],
                          128.0, ALU.add, waits=(g4, coef_tok, prev_sin),
                          inc=True)
                mk.drain("ve")
            else:
                v = coef_tok
                mk.wait("te", g4)
            mmt = None
            for seg in range(3):
                sl = psa.ap()[:, seg * 512:(seg + 1) * 512]
                mk.op("te", "matmul", sl, tposr[:, :],
                      coefrow.ap()[0:1, seg * 512:(seg + 1) * 512],
                      start=True, stop=False, waits=(v, prev_sin))
                mmt = mk.op("te", "matmul", sl, c["ones_row"][:, :],
                            shiftrow.ap()[0:1, seg * 512:(seg + 1) * 512],
                            start=False, stop=True, inc=(seg == 2))
            v1 = mk.op("ve", "tensor_copy", angi[:, :], psa[:, :],
                       waits=(mmt, prev_store), inc=True)
            mk.drain("ve")
            v2 = mk.op("ve", "tensor_copy", angf[:, :], angi[:, :], inc=True)
            mk.drain("ve")
            v3 = mk.op("ve", "tensor_sub", angf[:, :], psa[:, :], angf[:, :],
                       inc=True)
            mk.drain("ve")
            v3 = mk.op("ve", "tensor_single_scalar", angc[:, :], angf[:, :],
                       0.5, ALU.is_gt, inc=True)
            mk.drain("ve")
            v3 = mk.op("ve", "tensor_sub", angf[:, :], angf[:, :], angc[:, :],
                       inc=True)
            prev_sin = mk.op("se", "activation", rtab[:, :], angf[:, :], AF.Sin,
                             scale=2.0 * math.pi, waits=(v3,), inc=True)
            prev_store = mk.dma("sy", ROPET[tt * 128:(tt + 1) * 128, :],
                                rtab[:, :], waits=(prev_sin,), stream="p0rt")
        mk.barrier()
        mk.flush_block()


def p1_qkv(B, c):
    nc, mk = B.nc, B.mk
    inp = c["inp"]
    Qo, Ko, Vo, ROPET = c["Qo"], c["Ko"], c["Vo"], c["ROPET"]
    with ExitStack() as st:
        sb = lambda n, sh, dt: B.sb(st, n, sh, dt)
        xts = [sb(f"xts{i}", [128, 16 * 1024], BF16) for i in range(2)]
        rtabs = [sb(f"rtabs{i}", [128, 8 * ROPE_W], BF16) for i in range(2)]
        wsl = [sb(f"wsl{i}", [128, 16 * 512], BF16) for i in range(2)]
        cstage = [sb(f"cst{i}", [128, 512], F32) for i in range(2)]
        rt1 = sb("rt1", [128, 256], F32)
        rt2 = sb("rt2", [128, 256], F32)
        qkst = [sb(f"qkst{i}", [128, 512], BF16) for i in range(3)]
        vst = [sb(f"vst{i}", [128, 516], BF16) for i in range(3)]
        nq = sb("nq", [128, 16], F32)
        nk = sb("nk", [128, 16], F32)
        nqt = sb("nqt", [128, 4], F32)
        pp = [st.enter_context(nc.psum_tensor(f"pp{i}", [128, 512], F32))
              for i in range(2)]

        z0 = mk.op("ve", "memset", nq[:, :], 0.0, inc=True)
        z0 = mk.op("ve", "memset", nk[:, :], 0.0, inc=True)
        vz = [mk.op("ve", "memset", vst[i][:, :], 1.0, inc=True)
              for i in range(3)]

        ev_tok = {0: None, 1: None}
        qk_st = {0: vz[0], 1: vz[1], 2: vz[2]}
        v_st = {0: vz[0], 1: vz[1], 2: vz[2]}
        wl_gate = None
        np_used = {0: None, 1: None}
        nacc_tok = {0: z0, 2: z0}

        def issue_qloads(qr, gate):
            qsl = qr % 2
            qoff = qr * 1024
            xt = mk.dma(
                "gp", rearr(xts[qsl].ap(), "p (ec t) -> p ec t", ec=16),
                rearr(inp["xbT"][:, qoff:qoff + 1024],
                      "(ec p) t -> p ec t", p=128),
                waits=(gate,), stream="xts")
            rt = mk.dma(
                "sy", rearr(rtabs[qsl].ap(), "p (tt w) -> p tt w", tt=8),
                rearr(ROPET[qoff:qoff + 1024, :], "(tt p) w -> p tt w", p=128),
                waits=(gate,), stream="rtb1")
            return (xt, rt)

        qload_tok = {0: issue_qloads(0, None)}
        for qr in range(4):
            qsl = qr % 2
            xt_tok, rt_tok = qload_tok[qr]
            first_q_mm = None
            for fc in range(12):
                slot = fc % 2
                wl_tok = mk.dma(
                    "gp", rearr(wsl[slot].ap(), "p (ec f) -> p ec f", ec=16),
                    rearr(c["WQB"][:, fc * 512:(fc + 1) * 512],
                          "(ec p) f -> p ec f", p=128),
                    waits=(wl_gate, c["wcq_tok"]), stream="wq")
                g, kind = divmod(fc, 3)
                for tt in range(8):
                    gtt = qr * 8 + tt
                    pslot = tt % 2
                    w0 = [wl_tok, ev_tok[pslot], xt_tok]
                    mmtok = None
                    for ec in range(16):
                        mmtok = mk.op(
                            "te", "matmul", pp[pslot][:, :],
                            xts[qsl].ap()[:, ec * 1024 + tt * 128:
                                          ec * 1024 + (tt + 1) * 128],
                            wsl[slot].ap()[:, ec * 512:(ec + 1) * 512],
                            start=(ec == 0), stop=(ec == 15),
                            waits=w0 if ec == 0 else (),
                            inc=(ec == 15 or (ec == 0 and tt == 0)))
                    if tt == 0:
                        wl_gate = (mk.esem["te"], mk.esem["te"].n - 1)
                        if fc == 0:
                            first_q_mm = wl_gate
                    if kind == 1:
                        vslot = tt % 3
                        ev = mk.op(
                            "ve", "tensor_copy",
                            rearr(vst[vslot].ap(), "p (h x) -> p h x", h=4)
                            [:, :, 0:128],
                            rearr(pp[pslot].ap(), "p (h x) -> p h x", h=4),
                            waits=(mmtok, v_st[vslot]), inc=True)
                        ev_tok[pslot] = ev
                        v_st[vslot] = mk.dma(
                            "sy", Vo[gtt * 128:(gtt + 1) * 128,
                                     g * 516:(g + 1) * 516],
                            vst[vslot][:, :], waits=(ev,), stream="stv")
                    else:
                        ev = mk.op("ve", "tensor_copy", cstage[pslot][:, :],
                                   pp[pslot][:, :], waits=(mmtok,), inc=True)
                        sq = mk.op("se", "activation", c["junkb"][:, 0:512],
                                   pp[pslot][:, :], AF.Square,
                                   waits=(mmtok, ev, np_used[0]), inc=True)
                        ev_tok[pslot] = sq
                        accdst = nq if kind == 0 else nk
                        for hh in range(4):
                            mk.op("ve", "reduce_sum", nqt[:, hh:hh + 1],
                                  c["junkb"][:, hh * 128:(hh + 1) * 128], AX.X,
                                  waits=(sq,) if hh == 0 else ())
                        mk.drain("ve")
                        na = mk.op("ve", "tensor_max",
                                   accdst[:, g * 4:(g + 1) * 4],
                                   accdst[:, g * 4:(g + 1) * 4], nqt[:, :],
                                   waits=(nacc_tok[kind],), inc=True)
                        nacc_tok[kind] = na
                        np_used[0] = na
                        oslot = (fc + tt) % 3
                        cs4 = rearr(cstage[pslot].ap(),
                                    "p (h i t) -> p h i t", h=4, t=2)
                        base = tt * ROPE_W
                        C4 = rearr(rtabs[qsl].ap()[:, base:base + 256],
                                   "p (h i t) -> p h i t", h=4, t=2)
                        S4 = rearr(rtabs[qsl].ap()[:, base + 256:base + 512],
                                   "p (h i t) -> p h i t", h=4, t=2)
                        r1 = rearr(rt1.ap(), "p (h i t) -> p h i t", h=4, t=2)
                        r2 = rearr(rt2.ap(), "p (h i t) -> p h i t", h=4, t=2)
                        o4 = rearr(qkst[oslot].ap(),
                                   "p (h i t) -> p h i t", h=4, t=2)
                        v = mk.op("ve", "tensor_tensor", r1[:, :, 0:32, :],
                                  cs4[:, :, 0:32, :], C4[:, :, 0:32, :],
                                  ALU.mult, waits=(ev, rt_tok, qk_st[oslot]),
                                  inc=True)
                        v = mk.op("ve", "tensor_tensor", r2[:, :, 0:32, 0:1],
                                  cs4[:, :, 0:32, 1:2], S4[:, :, 0:32, 0:1],
                                  ALU.mult, inc=True)
                        v = mk.op("ve", "tensor_tensor", r2[:, :, 0:32, 1:2],
                                  cs4[:, :, 0:32, 0:1], S4[:, :, 0:32, 1:2],
                                  ALU.mult, inc=True)
                        mk.drain("ve")
                        v = mk.op("ve", "tensor_add", o4[:, :, 0:32, :],
                                  r1[:, :, 0:32, :], r2[:, :, 0:32, :], inc=True)
                        v = mk.op("ve", "tensor_copy", o4[:, :, 32:64, :],
                                  cs4[:, :, 32:64, :], inc=True)
                        dstD = Qo if kind == 0 else Ko
                        qk_st[oslot] = mk.dma(
                            "sy", dstD[gtt * 128:(gtt + 1) * 128,
                                       g * 512:(g + 1) * 512],
                            qkst[oslot][:, :], waits=(v,), stream="stqk")
                if fc == 0 and qr + 1 < 4:
                    qload_tok[qr + 1] = issue_qloads(qr + 1, first_q_mm)
        finalize_bound(B, c, st, nq, nk, nacc_tok[0], nacc_tok[2],
                       c["biasO"], rowbase=0)
        mk.barrier()
        mk.flush_block()


def finalize_bound(B, c, st, nqacc, nkacc, tok_q, tok_k, bias_dst, rowbase):
    nc, mk = B.nc, B.mk
    NRMD = c["NRMD"]
    rows = B.sb(st, f"rows{rowbase}", [16, 128], F32)
    qcol = B.sb(st, f"qcol{rowbase}", [16, 2], F32)
    mrow = B.sb(st, f"mrow{rowbase}", [1, 16], F32)
    psb = st.enter_context(nc.psum_tensor(f"psbb{rowbase}", [128, 16], F32))
    toks = []
    for h in range(16):
        toks.append(mk.dma("sy", rearr(NRMD[rowbase + h:rowbase + h + 1, :],
                                       "o p -> p o"),
                           nqacc[:, h:h + 1], waits=(tok_q,), stream="nrm"))
        toks.append(mk.dma("sy", rearr(NRMD[rowbase + 16 + h:rowbase + 17 + h, :],
                                       "o p -> p o"),
                           nkacc[:, h:h + 1], waits=(tok_k,), stream="nrm"))
    t1 = mk.dma("sy", rows[:, :], NRMD[rowbase:rowbase + 16, :], waits=toks, stream="nrm")
    v = mk.op("ve", "reduce_max", qcol[:, 0:1], rows[:, :], AX.X,
              waits=(t1,), inc=True)
    t2 = mk.dma("sy", rows[:, :], NRMD[rowbase + 16:rowbase + 32, :],
                waits=(t1, v), stream="nrm")
    v = mk.op("ve", "reduce_max", qcol[:, 1:2], rows[:, :], AX.X,
              waits=(t2,), inc=True)
    mk.drain("ve")
    v = mk.op("ve", "tensor_add", qcol[:, 0:1], qcol[:, 0:1], qcol[:, 1:2],
              inc=True)
    mk.drain("ve")
    v = mk.op("ve", "tensor_single_scalar", qcol[:, 0:1], qcol[:, 0:1],
              -1.02 / (2.0 * math.sqrt(128.0)), ALU.mult, inc=True)
    t3 = mk.dma("sy", rearr(NRMD[rowbase:rowbase + 1, 0:16], "o p -> p o"),
                qcol[0:16, 0:1], waits=(v,), stream="nrm")
    t4 = mk.dma("sy", mrow[:, :], NRMD[rowbase:rowbase + 1, 0:16], waits=(t3,), stream="nrm")
    t5 = mk.op("te", "matmul", psb[:, :], c["ones_row"][:, :], mrow[:, :],
               start=True, stop=True, waits=(t4,), inc=True)
    mk.op("ve", "tensor_copy", bias_dst[:, :], psb[:, :], waits=(t5,), inc=True)


def flash(B, c, outer):
    nc, mk = B.nc, B.mk
    if outer:
        Qsrc, Ksrc, Vsrc, Odst = c["Qo"], c["Ko"], c["Vo"], c["O1"]
        bias, SQ, nqc = c["biasO"], S, 8
    else:
        Qsrc, Ksrc, Vsrc, Odst = c["Q2"], c["K2"], c["V2"], c["O2d"]
        bias, SQ, nqc = c["bias2"], OWN, 2
    tag = "o" if outer else "i"
    with ExitStack() as st:
        sb = lambda n, sh, dt: B.sb(st, n + tag, sh, dt)
        iorow = sb("iorow", [128, MROW], F32)
        master = sb("master", [128, MROW], BF16)
        g1 = mk.op("gp", "iota", iorow[:, :], [[1, MROW]], channel_multiplier=0,
                   allow_small_or_imprecise_dtypes=True, inc=True)
        v = mk.op("ve", "tensor_scalar", iorow[:, :], iorow[:, :],
                  c["iota_col"][:, :], None, ALU.subtract, waits=(g1,), inc=True)
        mk.drain("ve")
        if outer:
            mtok = mk.op("ve", "tensor_single_scalar", master[:, :], iorow[:, :],
                         float(MOFF), ALU.is_ge, inc=True)
        else:
            thr = sb("thr", [128, 1], F32)
            v = mk.op("ve", "tensor_scalar", thr[:, :], c["qstart_c"][:, :],
                      -1.0, float(MOFF), ALU.mult, ALU.add, inc=True)
            mk.drain("ve")
            mtok = mk.op("ve", "tensor_scalar", master[:, :], iorow[:, :],
                         thr[:, :], None, ALU.is_ge, inc=True)

        KhT = [sb(f"KhT{i}", [128, S], BF16) for i in range(2)]
        QhT = [sb(f"QhT{i}", [128, SQ], BF16) for i in range(2)]
        Vh = [sb(f"Vh{i}", [128, 32 * 129], BF16) for i in range(2)]
        PT = [sb(f"PT{i}", [128, 1024], BF16) for i in range(3)]
        fst = [sb(f"fst{i}", [128, 512], BF16) for i in range(2)]
        linv = sb("linv", [128, 4], F32)
        stp = [st.enter_context(nc.psum_tensor(f"stp{tag}{i}", [128, 1024], F32))
               for i in range(2)]
        otp = [st.enter_context(nc.psum_tensor(f"otp{tag}{i}", [128, 258], F32))
               for i in range(4)]

        load_tok = {}
        exp_tok = {0: None, 1: None}
        ptfree = {0: None, 1: None, 2: None}
        ot_free = {0: None, 1: None}
        fs_tok = {0: None, 1: None}

        def issue_loads(h, gate=None):
            hb = h % 2
            kt_ = mk.dmat(KhT[hb][:, :], Ksrc[:, h * 128:(h + 1) * 128],
                          waits=(gate,), stream="fkq")
            qt_ = mk.dmat(QhT[hb][:, :], Qsrc[:, h * 128:(h + 1) * 128],
                          stream="fkq")
            vt_ = mk.dma("sy", rearr(Vh[hb].ap(), "p (kt v) -> p kt v", kt=32),
                         rearr(Vsrc[:, h * 129:(h + 1) * 129],
                               "(kt p) v -> p kt v", p=128), stream="fv")
            load_tok[h] = (kt_, qt_, vt_)

        issue_loads(0)
        for h in range(NH):
            hb = h % 2
            if h + 1 < NH:
                pass
            ktok, qtok, vtok = load_tok[h]
            last_pv = None
            first_pv1 = None
            for qc in range(nqc):
                nkt = 4 * (qc + 1) if outer else 32
                npairs = nkt // 2
                def emit_pv(pp_, ptok_):
                    nonlocal last_pv, first_pv1
                    ss_ = pp_ % 3
                    ob_ = 2 * (qc % 2)
                    pv = None
                    for half in range(2):
                        kt = 2 * pp_ + half
                        for qs in range(4):
                            oi, col = qs // 2, (qs % 2) * 129
                            first = (pp_ == 0 and qc == 0 and half == 0
                                     and qs == 0)
                            pre_n2 = mk.esem["te"].n
                            pv = mk.op(
                                "te", "matmul", otp[ob_ + oi][:, col:col + 129],
                                PT[ss_][:, half * 512 + qs * 128:
                                        half * 512 + (qs + 1) * 128],
                                Vh[hb][:, kt * 129:(kt + 1) * 129],
                                start=(pp_ == 0 and half == 0 and qs % 2 == 0),
                                stop=(pp_ == npairs - 1 and half == 1
                                      and qs % 2 == 1),
                                skip_group_check=True,
                                waits=(ptok_, vtok, ot_free[qc % 2])
                                if (qs == 0 and half == 0) else (),
                                inc=((half == 1 and qs == 3) or first))
                            if first:
                                first_pv1 = (mk.esem["te"], pre_n2 + 1)
                    ptfree[ss_] = pv
                    last_pv = pv

                ptoks = {}
                for p in range(npairs):
                    ss = p % 3
                    ssq = p % 2
                    mm1 = None
                    for half in range(2):
                        kt = 2 * p + half
                        pre_n = mk.esem["te"].n
                        mm1 = mk.op(
                            "te", "matmul", stp[ssq][:, half * 512:(half + 1) * 512],
                            KhT[hb][:, kt * 128:(kt + 1) * 128],
                            QhT[hb][:, qc * 512:(qc + 1) * 512],
                            start=True, stop=True,
                            waits=(exp_tok[ssq], ktok, qtok, mtok) if half == 0
                            else (),
                            inc=(half == 1 or (p == 0 and qc == 0
                                               and half == 0)))
                        if p == 0 and qc == 0 and half == 0:
                            first_mm1 = (mk.esem["te"], pre_n + 1)
                    ex = mk.op("se", "activation", PT[ss][:, :], stp[ssq][:, :],
                               AF.Exp, bias=bias[:, h:h + 1], scale=ISC,
                               waits=(mm1, ptfree[ss]), inc=True)
                    exp_tok[ssq] = ex
                    ptok = ex
                    for half in range(2):
                        kt = 2 * p + half
                        if outer:
                            dj = kt - 4 * qc
                            masked = 0 <= dj < 4
                            off = MOFF - dj * 128
                        else:
                            masked = True
                            off = qc * 512 - kt * 128 + MOFF
                        if masked:
                            ptok = mk.op(
                                "ve", "tensor_tensor",
                                PT[ss][:, half * 512:(half + 1) * 512],
                                PT[ss][:, half * 512:(half + 1) * 512],
                                master[:, off:off + 512], ALU.mult,
                                waits=(ex,), inc=True)
                    ptoks[p] = ptok
                    if p >= 1:
                        emit_pv(p - 1, ptoks.pop(p - 1))
                emit_pv(npairs - 1, ptoks.pop(npairs - 1))
                if qc == 0 and h + 1 < NH:
                    issue_loads(h + 1, gate=first_pv1)
                fs = qc % 2
                for qs in range(4):
                    oi, col = qs // 2, (qs % 2) * 129
                    mk.op("ve", "reciprocal", linv[:, qs:qs + 1],
                          otp[2 * fs + oi][:, col + 128:col + 129],
                          waits=(last_pv, fs_tok[fs]) if qs == 0 else ())
                mk.drain("ve")
                ev = None
                for qs in range(4):
                    oi, col = qs // 2, (qs % 2) * 129
                    ev = mk.op("ve", "tensor_scalar",
                               fst[fs][:, qs * 128:(qs + 1) * 128],
                               otp[2 * fs + oi][:, col:col + 128],
                               linv[:, qs:qs + 1],
                               None, ALU.mult, inc=(qs == 3))
                ot_free[fs] = ev
                fs_tok[fs] = mk.dma(
                    "gp", rearr(Odst[qc * 512:(qc + 1) * 512,
                                     h * 128:(h + 1) * 128],
                                "(qs p) d -> p qs d", p=128),
                    rearr(fst[fs].ap(), "p (qs d) -> p qs d", qs=4),
                    waits=(ev,), stream="fo")

        mk.barrier()
        mk.flush_block()



def p3_outproj(B, c):
    nc, mk = B.nc, B.mk
    inp, O1, X1, Zd = c["inp"], c["O1"], c["X1"], c["Zd"]
    with ExitStack() as st:
        sb = lambda n, sh, dt: B.sb(st, n, sh, dt)
        wsl = sb("wslo", [128, 16 * 2048], BF16)
        o1t = [sb(f"o1t{i}", [128, 16 * 512], BF16) for i in range(2)]
        xbt = [sb(f"xbt{i}", [128, 2048], F32) for i in range(2)]
        x1s = [sb(f"x1s{i}", [128, 2048], F32) for i in range(2)]
        zs = [sb(f"zs{i}", [128, 2048], BF16) for i in range(2)]
        cols = [sb(f"cols{i}", [128, 8], F32) for i in range(2)]
        pp = [st.enter_context(nc.psum_tensor(f"pp3{i}", [128, 512], F32))
              for i in range(2)]
        wtok = []
        for piece in range(4):
            sl = slice(piece * 4, (piece + 1) * 4)
            wtok.append(mk.dma(
                "gp", rearr(wsl.ap(), "p (dc f) -> p dc f", dc=16)[:, sl, :],
                rearr(inp["woutT"], "(dc p) f -> p dc f", p=128)[:, sl, :],
                stream="wo3"))
        o1_used = {0: None, 1: None}
        ev_done = {0: None, 1: None}
        x1_used = {0: (), 1: ()}
        for tg in range(8):
            sl_ = tg % 2
            ot = []
            for dc in range(16):
                ot.append(mk.dmat(o1t[sl_][:, dc * 512:(dc + 1) * 512],
                                  O1[tg * 512:(tg + 1) * 512,
                                     dc * 128:(dc + 1) * 128],
                                  waits=(o1_used[sl_],) if dc == 0 else (),
                                  stream="o1t"))
            last_mm = None
            for t4 in range(4):
                tt = tg * 4 + t4
                xt = mk.dma("sy", xbt[t4 % 2][:, :],
                            inp["xb"][tt * 128:(tt + 1) * 128, :],
                            waits=x1_used[t4 % 2] or (), stream="xb3")
                evl = None
                mmt = None
                for fc in range(4):
                    pslot = fc % 2
                    for dc in range(16):
                        mmt = mk.op(
                            "te", "matmul", pp[pslot][:, :],
                            o1t[sl_][:, dc * 512 + t4 * 128:
                                     dc * 512 + (t4 + 1) * 128],
                            wsl.ap()[:, dc * 2048 + fc * 512:
                                     dc * 2048 + (fc + 1) * 512],
                            start=(dc == 0), stop=(dc == 15),
                            waits=tuple(ot) + tuple(wtok) + (ev_done[pslot],)
                            if dc == 0 else (), inc=(dc == 15))
                    evl = mk.op("ve", "scalar_tensor_tensor",
                                x1s[t4 % 2][:, fc * 512:(fc + 1) * 512],
                                pp[pslot][:, :], 0.0,
                                xbt[t4 % 2][:, fc * 512:(fc + 1) * 512],
                                ALU.bypass, ALU.add, waits=(mmt, xt), inc=True)
                    ev_done[pslot] = evl
                last_mm = mmt
                x1_used[t4 % 2] = ln_z_epilogue(
                    B, c, x1s[t4 % 2], zs[t4 % 2], cols[t4 % 2], evl,
                    X1[tt * 128:(tt + 1) * 128, :],
                    Zd[tt * 128:(tt + 1) * 128, :])
            o1_used[sl_] = last_mm
        mk.barrier()
        mk.flush_block()


def ln_z_epilogue(B, c, x1s, zs, cl, evl, Xrow, Zrow):
    nc, mk = B.nc, B.mk
    sq = mk.op("se", "activation", c["junkb"][:, :], x1s[:, :],
               AF.Square, accum_out=cl[:, 0:1], waits=(evl,), inc=True)
    v = mk.op("ve", "reduce_sum", cl[:, 1:2], x1s[:, :], AX.X,
              waits=(evl,), inc=True)
    mk.drain("ve")
    v = mk.op("ve", "tensor_single_scalar", cl[:, 1:2], cl[:, 1:2],
              1.0 / D, ALU.mult, waits=(sq,), inc=True)
    mk.drain("ve")
    v = mk.op("ve", "tensor_tensor", cl[:, 2:3], cl[:, 1:2], cl[:, 1:2],
              ALU.mult, inc=True)
    v = mk.op("ve", "tensor_scalar", cl[:, 3:4], cl[:, 0:1], 1.0 / D,
              None, ALU.mult, inc=True)
    mk.drain("ve")
    v = mk.op("ve", "tensor_sub", cl[:, 3:4], cl[:, 3:4], cl[:, 2:3],
              inc=True)
    sqv = mk.op("se", "activation", cl[:, 4:5], cl[:, 3:4], AF.Sqrt,
                bias=c["eps_col"][:, :], waits=(v,), inc=True)
    v = mk.op("ve", "reciprocal", cl[:, 5:6], cl[:, 4:5],
              waits=(sqv,), inc=True)
    mk.drain("ve")
    v = mk.op("ve", "tensor_tensor", cl[:, 6:7], cl[:, 1:2], cl[:, 5:6],
              ALU.mult, inc=True)
    mk.drain("ve")
    v = mk.op("ve", "tensor_single_scalar", cl[:, 6:7], cl[:, 6:7],
              -1.0, ALU.mult, inc=True)
    mk.drain("ve")
    v = mk.op("ve", "tensor_scalar", x1s[:, :], x1s[:, :],
              cl[:, 5:6], cl[:, 6:7], ALU.mult, ALU.add, inc=True)
    tx = mk.dma("gp", Xrow, x1s[:, :], waits=(v,), stream="sx1")
    v2 = mk.op("ve", "tensor_tensor", cl[:, 7:8], cl[:, 3:4], cl[:, 5:6],
               ALU.mult, waits=(v,), inc=True)
    mk.drain("ve")
    v2 = mk.op("ve", "tensor_tensor", cl[:, 7:8], cl[:, 7:8], cl[:, 5:6],
               ALU.mult, inc=True)
    sqv = mk.op("se", "activation", cl[:, 7:8], cl[:, 7:8], AF.Sqrt,
                bias=c["eps_col"][:, :], waits=(v2,), inc=True)
    v2 = mk.op("ve", "reciprocal", cl[:, 7:8], cl[:, 7:8],
               waits=(sqv,), inc=True)
    mk.drain("ve")
    v2 = mk.op("ve", "tensor_scalar", zs[:, :], x1s[:, :],
               cl[:, 7:8], None, ALU.mult, inc=True)
    tz = mk.dma("gp", Zrow, zs[:, :], waits=(v2,), stream="szd")
    return (tx, tz, v2)


def p5_mlakv(B, c):
    nc, mk = B.nc, B.mk
    inp = c["inp"]
    Zd, KVD, K2, V2, ROPET = c["Zd"], c["KVD"], c["K2"], c["V2"], c["ROPET"]
    with ExitStack() as st:
        sb = lambda n, sh, dt: B.sb(st, n, sh, dt)
        kvdw_s = sb("kvdw_s", [128, 16 * 512], BF16)
        ropekw_s = sb("ropekw_s", [128, 16 * 1024], BF16)
        kuw_s = sb("kuw_s", [128, 4 * 1024], BF16)
        vuw_s = sb("vuw_s", [128, 4 * 2048], BF16)
        zts = [sb(f"zts{i}", [128, 16 * 512], BF16) for i in range(2)]
        kvdT = sb("kvdT", [128, 4 * 512], BF16)
        kst = [sb(f"kst{i}", [128, 2048], BF16) for i in range(2)]
        v2st = [sb(f"v2st{i}", [128, V129], BF16) for i in range(2)]
        crm = sb("crm", [128, 512], F32)
        r1m = sb("r1m", [128, 512], F32)
        r2m = sb("r2m", [128, 512], F32)
        rtabm = [sb(f"rtabm{i}", [128, 1024], BF16) for i in range(2)]
        nk2t = sb("nk2t", [128, 16], F32)
        nred = sb("nred", [128, 8], F32)
        pp = [st.enter_context(nc.psum_tensor(f"pp5{i}", [128, 512], F32))
              for i in range(2)]

        wt1 = mk.dma("gp", rearr(kvdw_s.ap(), "p (ec f) -> p ec f", ec=16),
                     rearr(inp["kvdw"], "(ec p) f -> p ec f", p=128), stream="w5")
        wt2 = mk.dma("gp", rearr(ropekw_s.ap(), "p (ec f) -> p ec f", ec=16),
                     rearr(inp["ropekw"], "(ec p) f -> p ec f", p=128), stream="w5")
        wt3 = mk.dma("gp", rearr(kuw_s.ap(), "p (lc f) -> p lc f", lc=4),
                     rearr(inp["kuw"], "(lc p) f -> p lc f", p=128), stream="w5")
        wt4 = mk.dma("gp", rearr(vuw_s.ap(), "p (lc f) -> p lc f", lc=4),
                     rearr(inp["vuw"], "(lc p) f -> p lc f", p=128), stream="w5")
        z0 = mk.op("ve", "memset", c["nk2g"][:, :], 0.0, inc=True)
        vz = [mk.op("ve", "memset", v2st[i][:, :], 1.0, inc=True)
              for i in range(2)]
        zt_used = {0: None, 1: None}
        kv_used = {0: None, 1: None}
        kst_tok = {0: vz[0], 1: vz[1]}
        v2_tok = {0: vz[0], 1: vz[1]}
        rt_used = {0: None, 1: None}
        ev_done = {0: None, 1: None}
        nk_tok = z0
        jfree = None
        kvdT_used = None
        for tc in range(8):
            zsl = tc % 2
            ztok = []
            for ec in range(16):
                ztok.append(mk.dmat(
                    zts[zsl][:, ec * 512:(ec + 1) * 512],
                    Zd[tc * 512:(tc + 1) * 512, ec * 128:(ec + 1) * 128],
                    waits=(zt_used[zsl],) if ec == 0 else (), stream="zts"))
            gv = None
            for ec in range(16):
                gv = mk.op("ve", "tensor_scalar",
                           zts[zsl][:, ec * 512:(ec + 1) * 512],
                           zts[zsl][:, ec * 512:(ec + 1) * 512],
                           c["ln1c"][:, ec:ec + 1], None, ALU.mult,
                           waits=(ztok[ec],), inc=(ec == 15))
            ev4 = []
            for lc in range(4):
                ps = lc % 2
                mmt = None
                for ec in range(16):
                    mmt = mk.op("te", "matmul", pp[ps][:, :],
                                kvdw_s.ap()[:, ec * 512 + lc * 128:
                                            ec * 512 + (lc + 1) * 128],
                                zts[zsl][:, ec * 512:(ec + 1) * 512],
                                start=(ec == 0), stop=(ec == 15),
                                waits=(gv, ev_done[ps], wt1, kvdT_used)
                                if ec == 0 else (), inc=(ec == 15))
                ev = mk.op("ve", "tensor_copy", kvdT[:, lc * 512:(lc + 1) * 512],
                           pp[ps][:, :], waits=(mmt,), inc=True)
                ev_done[ps] = ev
                ev4.append(ev)
            ktok = tuple(ev4)
            for t4 in range(4):
                tt = tc * 4 + t4
                ksl = t4 % 2
                rsl = t4 % 2
                rtok = mk.dma("sy", rtabm[rsl][:, :],
                              ROPET[tt * 128:(tt + 1) * 128, 512:1536],
                              waits=(rt_used[rsl],), stream="rtm")
                k3 = rearr(kst[ksl].ap(), "p (h x) -> p h x", h=16)
                nkw = None
                for fc in range(2):
                    ps = fc % 2
                    mmt = None
                    for lc in range(4):
                        mmt = mk.op("te", "matmul", pp[ps][:, :],
                                    kvdT[:, lc * 512 + t4 * 128:
                                         lc * 512 + (t4 + 1) * 128],
                                    kuw_s.ap()[:, lc * 1024 + fc * 512:
                                               lc * 1024 + (fc + 1) * 512],
                                    start=(lc == 0), stop=(lc == 3),
                                    waits=ktok + (ev_done[ps], wt3)
                                    if lc == 0 else (), inc=(lc == 3))
                    ev = mk.op("ve", "tensor_copy",
                               k3[:, fc * 8:(fc + 1) * 8, 0:64],
                               rearr(pp[ps].ap(), "p (h i) -> p h i", h=8),
                               waits=(mmt, kst_tok[ksl]), inc=True)
                    sq = mk.op("se", "activation", c["junkb"][:, 0:512],
                               pp[ps][:, :], AF.Square,
                               waits=(mmt, ev, jfree), inc=True)
                    ev_done[ps] = sq
                    for hh in range(8):
                        jfree = mk.op("ve", "reduce_sum",
                                      nk2t[:, fc * 8 + hh:fc * 8 + hh + 1],
                                      c["junkb"][:, hh * 64:(hh + 1) * 64],
                                      AX.X, waits=(sq,) if hh == 0 else (),
                                      inc=(hh == 7))
                for fc in range(2):
                    ps = fc % 2
                    mmt = None
                    for ec in range(16):
                        mmt = mk.op("te", "matmul", pp[ps][:, :],
                                    zts[zsl][:, ec * 512 + t4 * 128:
                                             ec * 512 + (t4 + 1) * 128],
                                    ropekw_s.ap()[:, ec * 1024 + fc * 512:
                                                  ec * 1024 + (fc + 1) * 512],
                                    start=(ec == 0), stop=(ec == 15),
                                    waits=(ev_done[ps], wt2) if ec == 0 else (),
                                    inc=(ec == 15))
                    ev = mk.op("ve", "tensor_copy", crm[:, :], pp[ps][:, :],
                               waits=(mmt,), inc=True)
                    sq = mk.op("se", "activation", c["junkb"][:, 0:512],
                               pp[ps][:, :], AF.Square,
                               waits=(mmt, ev, jfree), inc=True)
                    ev_done[ps] = sq
                    for hh in range(8):
                        mk.op("ve", "reduce_sum", nred[:, hh:hh + 1],
                              c["junkb"][:, hh * 64:(hh + 1) * 64], AX.X,
                              waits=(sq,) if hh == 0 else ())
                    mk.drain("ve")
                    nkw = mk.op("ve", "tensor_add",
                                nk2t[:, fc * 8:(fc + 1) * 8],
                                nk2t[:, fc * 8:(fc + 1) * 8], nred[:, 0:8],
                                inc=True)
                    jfree = nkw
                    cr8 = rearr(crm.ap(), "p (h i) -> p h i", h=8)
                    C8 = rearr(rtabm[rsl].ap()[:, 0:512], "p (h i) -> p h i", h=8)
                    S8 = rearr(rtabm[rsl].ap()[:, 512:1024], "p (h i) -> p h i",
                               h=8)
                    R1 = rearr(r1m.ap(), "p (h i) -> p h i", h=8)
                    R2 = rearr(r2m.ap(), "p (h i) -> p h i", h=8)
                    v = mk.op("ve", "tensor_tensor", R1[:, :, :], cr8[:, :, :],
                              C8[:, :, :], ALU.mult, waits=(ev, rtok), inc=True)
                    v = mk.op("ve", "tensor_tensor", R2[:, :, 0:32],
                              cr8[:, :, 32:64], S8[:, :, 0:32], ALU.mult,
                              inc=True)
                    v = mk.op("ve", "tensor_tensor", R2[:, :, 32:64],
                              cr8[:, :, 0:32], S8[:, :, 32:64], ALU.mult,
                              inc=True)
                    mk.drain("ve")
                    v = mk.op("ve", "tensor_add",
                              k3[:, fc * 8:(fc + 1) * 8, 64:128],
                              R1[:, :, :], R2[:, :, :], inc=True)
                kst_tok[ksl] = mk.dma(
                    "gp", K2[tt * 128:(tt + 1) * 128, :], kst[ksl][:, :],
                    waits=(v,), stream="sk2")
                mk.drain("ve")
                nk_tok = mk.op("ve", "tensor_max", c["nk2g"][:, :],
                               c["nk2g"][:, :], nk2t[:, :],
                               waits=(nkw, nk_tok), inc=True)
                v3 = rearr(v2st[ksl].ap(), "p (h x) -> p h x", h=16)
                vv = None
                for fc in range(4):
                    ps = fc % 2
                    mmt = None
                    for lc in range(4):
                        mmt = mk.op("te", "matmul", pp[ps][:, :],
                                    kvdT[:, lc * 512 + t4 * 128:
                                         lc * 512 + (t4 + 1) * 128],
                                    vuw_s.ap()[:, lc * 2048 + fc * 512:
                                               lc * 2048 + (fc + 1) * 512],
                                    start=(lc == 0), stop=(lc == 3),
                                    waits=(ev_done[ps], wt4, v2_tok[ksl])
                                    if lc == 0 else (), inc=(lc == 3))
                    vv = mk.op("ve", "tensor_copy",
                               v3[:, fc * 4:(fc + 1) * 4, 0:128],
                               rearr(pp[ps].ap(), "p (h x) -> p h x", h=4),
                               waits=(mmt,), inc=True)
                    ev_done[ps] = vv
                v2_tok[ksl] = mk.dma(
                    "gp", V2[tt * 128:(tt + 1) * 128, :], v2st[ksl][:, :],
                    waits=(vv,), stream="sv2")
                rt_used[rsl] = v
            zt_used[zsl] = mmt
            kvdT_used = mmt
        mk.barrier()
        mk.flush_block()


def p6_gather(B, c):
    nc, mk = B.nc, B.mk
    Zd, X1, ZOWN, XO = c["Zd"], c["X1"], c["ZOWN"], c["XO"]
    with ExitStack() as st:
        sb = lambda n, sh, dt: B.sb(st, n, sh, dt)
        zg = sb("zg", [128, 4 * 2048], BF16)
        xg = sb("xg", [128, 4 * 2048], F32)
        zacc = sb("zacc", [128, 2048], BF16)
        xacc = sb("xacc", [128, 2048], F32)
        zst_tok = None
        xst_tok = None
        for tto in range(NTO):
            tz = mk.dma("sy", rearr(zg.ap(), "p (q f) -> p q f", q=4),
                        rearr(Zd[0:S, :], "(q r) f -> q r f", q=4)
                        [:, tto * 128:(tto + 1) * 128, :]
                        .rearrange("q p f -> p q f"),
                        waits=(zst_tok,), stream="gz")
            tx = mk.dma("sy", rearr(xg.ap(), "p (q f) -> p q f", q=4),
                        rearr(X1[0:S, :], "(q r) f -> q r f", q=4)
                        [:, tto * 128:(tto + 1) * 128, :]
                        .rearrange("q p f -> p q f"),
                        waits=(xst_tok,), stream="gx")
            v = mk.op("ve", "tensor_scalar", zacc[:, :], zg[:, 0:2048],
                      c["oselc"][:, 0:1], None, ALU.mult, waits=(tz,), inc=True)
            for q in range(1, 4):
                mk.drain("ve")
                v = mk.op("ve", "scalar_tensor_tensor", zacc[:, :],
                          zg[:, q * 2048:(q + 1) * 2048], c["oselc"][:, q:q + 1],
                          zacc[:, :], ALU.mult, ALU.add, inc=True)
            zst_tok = mk.dma("gp", ZOWN[tto * 128:(tto + 1) * 128, :],
                             zacc[:, :], waits=(v,), stream="szo")
            v = mk.op("ve", "tensor_scalar", xacc[:, :], xg[:, 0:2048],
                      c["oselc"][:, 0:1], None, ALU.mult, waits=(tx,), inc=True)
            for q in range(1, 4):
                mk.drain("ve")
                v = mk.op("ve", "scalar_tensor_tensor", xacc[:, :],
                          xg[:, q * 2048:(q + 1) * 2048], c["oselc"][:, q:q + 1],
                          xacc[:, :], ALU.mult, ALU.add, inc=True)
            xst_tok = mk.dma("gp", XO[tto * 128:(tto + 1) * 128, :],
                             xacc[:, :], waits=(v,), stream="sxo")
        mk.barrier()
        mk.flush_block()


def p7_qside(B, c):
    nc, mk = B.nc, B.mk
    inp = c["inp"]
    ZOWN, QD, Q2, ROPET = c["ZOWN"], c["QD"], c["Q2"], c["ROPET"]
    with ExitStack() as st:
        sb = lambda n, sh, dt: B.sb(st, n, sh, dt)
        qdw_s = sb("qdw_s", [128, 16 * 512], BF16)
        quw_s = sb("quw_s", [128, 4 * 1024], BF16)
        ropeqw_s = sb("ropeqw_s", [128, 4 * 1024], BF16)
        zoT = sb("zoT", [128, 16 * 512], BF16)
        qdT = sb("qdT", [128, 4 * 512], BF16)
        q2st = [sb(f"q2st{i}", [128, 2048], BF16) for i in range(2)]
        crm = sb("crmq", [128, 512], F32)
        r1m = sb("r1mq", [128, 512], F32)
        r2m = sb("r2mq", [128, 512], F32)
        rtq4 = sb("rtq4", [128, 4 * 1024], BF16)
        rtabq = sb("rtabq", [128, 1024], BF16)
        nq2t = sb("nq2t", [128, 16], F32)
        nred = sb("nredq", [128, 8], F32)
        nq2a = sb("nq2a", [128, 16], F32)
        pp = [st.enter_context(nc.psum_tensor(f"pp7{i}", [128, 512], F32))
              for i in range(2)]

        wt1 = mk.dma("gp", rearr(qdw_s.ap(), "p (ec f) -> p ec f", ec=16),
                     rearr(inp["qdw"], "(ec p) f -> p ec f", p=128), stream="w7")
        wt3 = mk.dma("gp", rearr(quw_s.ap(), "p (lc f) -> p lc f", lc=4),
                     rearr(inp["quw"], "(lc p) f -> p lc f", p=128), stream="w7")
        wt2 = mk.dma("gp", rearr(ropeqw_s.ap(), "p (lc f) -> p lc f", lc=4),
                     rearr(inp["ropeqw"], "(lc p) f -> p lc f", p=128),
                     stream="w7")
        z0 = mk.op("ve", "memset", nq2a[:, :], 0.0, inc=True)
        jfree = None
        qdT_used = None
        ev_done = {0: None, 1: None}
        kv_used = {0: None, 1: None}
        q2_tok = {0: None, 1: None}
        nk_tok = z0
        zt_used = None
        rt_used = None
        for tc in range(2):
            ztok = []
            for ec in range(16):
                ztok.append(mk.dmat(
                    zoT[:, ec * 512:(ec + 1) * 512],
                    ZOWN[tc * 512:(tc + 1) * 512, ec * 128:(ec + 1) * 128],
                    waits=(zt_used,) if ec == 0 else (), stream="zoT"))
            gv = None
            for ec in range(16):
                gv = mk.op("ve", "tensor_scalar",
                           zoT[:, ec * 512:(ec + 1) * 512],
                           zoT[:, ec * 512:(ec + 1) * 512],
                           c["ln1c"][:, ec:ec + 1], None, ALU.mult,
                           waits=(ztok[ec],), inc=(ec == 15))
            ev4 = []
            for lc in range(4):
                ps = lc % 2
                mmt = None
                for ec in range(16):
                    mmt = mk.op("te", "matmul", pp[ps][:, :],
                                qdw_s.ap()[:, ec * 512 + lc * 128:
                                           ec * 512 + (lc + 1) * 128],
                                zoT[:, ec * 512:(ec + 1) * 512],
                                start=(ec == 0), stop=(ec == 15),
                                waits=(gv, ev_done[ps], wt1, qdT_used)
                                if ec == 0 else (), inc=(ec == 15))
                ev = mk.op("ve", "tensor_copy", qdT[:, lc * 512:(lc + 1) * 512],
                           pp[ps][:, :], waits=(mmt,), inc=True)
                ev_done[ps] = ev
                ev4.append(ev)
            ktok = tuple(ev4)
            for t4 in range(4):
                tt = tc * 4 + t4
                ksl = t4 % 2
                trt = mk.dma("sy", rearr(rtq4.ap(), "p (q w) -> p q w", q=4),
                             rearr(ROPET[0:S, 512:1536], "(q r) w -> q r w", q=4)
                             [:, tt * 128:(tt + 1) * 128, :]
                             .rearrange("q p w -> p q w"),
                             waits=(rt_used,), stream="rtq")
                v = mk.op("ve", "tensor_scalar", rtabq[:, :], rtq4[:, 0:1024],
                          c["oselc"][:, 0:1], None, ALU.mult, waits=(trt,),
                          inc=True)
                for q in range(1, 4):
                    mk.drain("ve")
                    v = mk.op("ve", "scalar_tensor_tensor", rtabq[:, :],
                              rtq4[:, q * 1024:(q + 1) * 1024],
                              c["oselc"][:, q:q + 1], rtabq[:, :],
                              ALU.mult, ALU.add, inc=True)
                rtok = v
                k3 = rearr(q2st[ksl].ap(), "p (h x) -> p h x", h=16)
                for fc in range(2):
                    ps = fc % 2
                    mmt = None
                    for lc in range(4):
                        mmt = mk.op("te", "matmul", pp[ps][:, :],
                                    qdT[:, lc * 512 + t4 * 128:
                                        lc * 512 + (t4 + 1) * 128],
                                    quw_s.ap()[:, lc * 1024 + fc * 512:
                                               lc * 1024 + (fc + 1) * 512],
                                    start=(lc == 0), stop=(lc == 3),
                                    waits=ktok + (ev_done[ps], wt3,
                                                  q2_tok[ksl])
                                    if lc == 0 else (), inc=(lc == 3))
                    ev = mk.op("ve", "tensor_copy",
                               k3[:, fc * 8:(fc + 1) * 8, 0:64],
                               rearr(pp[ps].ap(), "p (h i) -> p h i", h=8),
                               waits=(mmt,), inc=True)
                    sq = mk.op("se", "activation", c["junkb"][:, 0:512],
                               pp[ps][:, :], AF.Square,
                               waits=(mmt, ev, jfree), inc=True)
                    ev_done[ps] = sq
                    for hh in range(8):
                        jfree = mk.op("ve", "reduce_sum",
                                      nq2t[:, fc * 8 + hh:fc * 8 + hh + 1],
                                      c["junkb"][:, hh * 64:(hh + 1) * 64],
                                      AX.X, waits=(sq,) if hh == 0 else (),
                                      inc=(hh == 7))
                nkw = None
                for fc in range(2):
                    ps = fc % 2
                    mmt = None
                    for lc in range(4):
                        mmt = mk.op("te", "matmul", pp[ps][:, :],
                                    qdT[:, lc * 512 + t4 * 128:
                                        lc * 512 + (t4 + 1) * 128],
                                    ropeqw_s.ap()[:, lc * 1024 + fc * 512:
                                                  lc * 1024 + (fc + 1) * 512],
                                    start=(lc == 0), stop=(lc == 3),
                                    waits=(ev_done[ps], wt2) if lc == 0 else (),
                                    inc=(lc == 3))
                    ev = mk.op("ve", "tensor_copy", crm[:, :], pp[ps][:, :],
                               waits=(mmt,), inc=True)
                    sq = mk.op("se", "activation", c["junkb"][:, 0:512],
                               pp[ps][:, :], AF.Square,
                               waits=(mmt, ev, jfree), inc=True)
                    ev_done[ps] = sq
                    for hh in range(8):
                        mk.op("ve", "reduce_sum", nred[:, hh:hh + 1],
                              c["junkb"][:, hh * 64:(hh + 1) * 64], AX.X,
                              waits=(sq,) if hh == 0 else ())
                    mk.drain("ve")
                    nkw = mk.op("ve", "tensor_add",
                                nq2t[:, fc * 8:(fc + 1) * 8],
                                nq2t[:, fc * 8:(fc + 1) * 8], nred[:, 0:8],
                                inc=True)
                    jfree = nkw
                    cr8 = rearr(crm.ap(), "p (h i) -> p h i", h=8)
                    C8 = rearr(rtabq.ap()[:, 0:512], "p (h i) -> p h i", h=8)
                    S8 = rearr(rtabq.ap()[:, 512:1024], "p (h i) -> p h i", h=8)
                    R1 = rearr(r1m.ap(), "p (h i) -> p h i", h=8)
                    R2 = rearr(r2m.ap(), "p (h i) -> p h i", h=8)
                    v = mk.op("ve", "tensor_tensor", R1[:, :, :], cr8[:, :, :],
                              C8[:, :, :], ALU.mult, waits=(ev, rtok), inc=True)
                    v = mk.op("ve", "tensor_tensor", R2[:, :, 0:32],
                              cr8[:, :, 32:64], S8[:, :, 0:32], ALU.mult,
                              inc=True)
                    v = mk.op("ve", "tensor_tensor", R2[:, :, 32:64],
                              cr8[:, :, 0:32], S8[:, :, 32:64], ALU.mult,
                              inc=True)
                    mk.drain("ve")
                    v = mk.op("ve", "tensor_add",
                              k3[:, fc * 8:(fc + 1) * 8, 64:128],
                              R1[:, :, :], R2[:, :, :], inc=True)
                q2_tok[ksl] = mk.dma(
                    "sy", Q2[tt * 128:(tt + 1) * 128, :], q2st[ksl][:, :],
                    waits=(v,), stream="sq2")
                rt_used = v
                qdT_used = mmt
                mk.drain("ve")
                nk_tok = mk.op("ve", "tensor_max", nq2a[:, :], nq2a[:, :],
                               nq2t[:, :], waits=(nkw, nk_tok), inc=True)
            zt_used = mmt
        finalize_bound(B, c, st, nq2a, c["nk2g"], nk_tok, nk_tok,
                       c["bias2"], rowbase=32)
        mk.barrier()
        mk.flush_block()


def p9_ow(B, c):
    nc, mk = B.nc, B.mk
    inp = c["inp"]
    O2d, XO, Z2, SP = c["O2d"], c["XO"], c["Z2"], c["SP"]
    with ExitStack() as st:
        sb = lambda n, sh, dt: B.sb(st, n, sh, dt)
        ow_s = sb("ow_s", [128, 16 * 2048], BF16)
        o2t = [sb(f"o2t{i}", [128, 16 * 512], BF16) for i in range(2)]
        xo = [sb(f"xo{i}", [128, 2048], F32) for i in range(2)]
        hs = [sb(f"hs{i}", [128, 2048], F32) for i in range(2)]
        z2st = [sb(f"z2st{i}", [128, 2048], BF16) for i in range(2)]
        spst = [sb(f"spst{i}", [128, 2048], F32) for i in range(2)]
        cols = [sb(f"cols9{i}", [128, 8], F32) for i in range(2)]
        pp = [st.enter_context(nc.psum_tensor(f"pp9{i}", [128, 512], F32))
              for i in range(2)]
        wtok = []
        for piece in range(4):
            sl = slice(piece * 4, (piece + 1) * 4)
            wtok.append(mk.dma(
                "gp", rearr(ow_s.ap(), "p (dc f) -> p dc f", dc=16)[:, sl, :],
                rearr(inp["ow"], "(dc p) f -> p dc f", p=128)[:, sl, :],
                stream="wo9"))
        o2_used = {0: None, 1: None}
        ev_done = {0: None, 1: None}
        buf_used = {0: (), 1: ()}
        ot_grp = {}
        for tto in range(NTO):
            sl_ = tto % 2
            if tto % 4 == 0:
                gsl = (tto // 4) % 2
                ot_grp[tto // 4] = [mk.dmat(
                    o2t[gsl][:, dc * 512:(dc + 1) * 512],
                    O2d[tto * 128:(tto + 4) * 128, dc * 128:(dc + 1) * 128],
                    waits=(o2_used[gsl],) if dc == 0 else (),
                    stream="o2t") for dc in range(16)]
            ot = ot_grp[tto // 4]
            xt = mk.dma("sy", xo[sl_][:, :], XO[tto * 128:(tto + 1) * 128, :],
                        waits=buf_used[sl_] or (), stream="xo9")
            evl = None
            mmt = None
            for fc in range(4):
                ps = fc % 2
                for dc in range(16):
                    mmt = mk.op(
                        "te", "matmul", pp[ps][:, :],
                        o2t[(tto // 4) % 2][:, dc * 512 + (tto % 4) * 128:
                                            dc * 512 + (tto % 4 + 1) * 128],
                        ow_s.ap()[:, dc * 2048 + fc * 512:
                                  dc * 2048 + (fc + 1) * 512],
                        start=(dc == 0), stop=(dc == 15),
                        waits=tuple(ot) + tuple(wtok) + (ev_done[ps],)
                        if dc == 0 else (), inc=(dc == 15))
                evl = mk.op("ve", "scalar_tensor_tensor",
                            hs[sl_][:, fc * 512:(fc + 1) * 512],
                            pp[ps][:, :], 0.0, xo[sl_][:, fc * 512:(fc + 1) * 512],
                            ALU.bypass, ALU.add, waits=(mmt, xt), inc=True)
                ev_done[ps] = evl
            if tto % 4 == 3:
                o2_used[(tto // 4) % 2] = mmt
            cl = cols[sl_]
            sq = mk.op("se", "activation", c["junkb"][:, :], hs[sl_][:, :],
                       AF.Square, accum_out=cl[:, 0:1], waits=(evl,), inc=True)
            v = mk.op("ve", "tensor_single_scalar", cl[:, 1:2], cl[:, 0:1],
                      1.0 / D, ALU.mult, waits=(sq,), inc=True)
            sq2 = mk.op("se", "activation", cl[:, 2:3], cl[:, 1:2], AF.Sqrt,
                        bias=c["eps_col"][:, :], waits=(v,), inc=True)
            v = mk.op("ve", "reciprocal", cl[:, 3:4], cl[:, 2:3],
                      waits=(sq2,), inc=True)
            mk.drain("ve")
            v = mk.op("ve", "tensor_scalar", z2st[sl_][:, :], hs[sl_][:, :],
                      cl[:, 3:4], None, ALU.mult, inc=True)
            tz = mk.dma("gp", Z2[tto * 128:(tto + 1) * 128, :], z2st[sl_][:, :],
                        waits=(v,), stream="sz2")
            v2 = mk.op("ve", "tensor_add", spst[sl_][:, :], hs[sl_][:, :],
                       xo[sl_][:, :], waits=(v,), inc=True)
            ts_ = mk.dma("gp", SP[tto * 128:(tto + 1) * 128, :], spst[sl_][:, :],
                         waits=(v2,), stream="ssp")
            buf_used[sl_] = (tz, ts_)
        mk.barrier()
        mk.flush_block()


def p10_moe(B, c):
    nc, mk = B.nc, B.mk
    inp = c["inp"]
    Z2, SP, out = c["Z2"], c["SP"], c["out"]
    with ExitStack() as st:
        sb = lambda n, sh, dt: B.sb(st, n, sh, dt)
        rw_s = sb("rw_s", [128, 16 * 8], BF16)
        rb_s = sb("rb_s", [1, NRE], BF16)
        z2T = sb("z2T", [128, 16 * 512], BF16)
        wg_s = [sb(f"wg_s{i}", [128, 16 * 512], BF16) for i in range(2)]
        wu_s = [sb(f"wu_s{i}", [128, 16 * 512], BF16) for i in range(2)]
        wd_s = [sb(f"wd_s{i}", [128, 4 * 2048], BF16) for i in range(2)]
        gbuf = [sb(f"gbuf{i}", [128, 8 * 512], BF16) for i in range(2)]
        sg = [sb(f"sgb{i}", [128, 512], BF16) for i in range(2)]
        su = [sb(f"sub{i}", [128, 512], BF16) for i in range(2)]
        wsb = sb("wsb", [128, 512], F32)
        wT = sb("wT", [8, 1024], BF16)
        wrow = sb("wrow", [1, 512], BF16)
        probs = sb("probs", [128, 8], F32)
        rcols = sb("rcols", [128, 8], F32)
        eqs = sb("eqs", [128, 8], F32)
        p2s = sb("p2s", [128, 8], F32)
        wts = sb("wts", [128, 8], F32)
        wtb = sb("wtb", [128, 8], BF16)
        moeacc = sb("moeacc", [128, 4 * 2048], F32)
        fin = [sb(f"fin{i}", [128, 2048], F32) for i in range(2)]
        cols = [sb(f"cols10{i}", [128, 8], F32) for i in range(2)]

        t_rw = mk.dma("gp", rearr(rw_s.ap(), "p (ec f) -> p ec f", ec=16)
                      [:, :, 0:NRE],
                      rearr(inp["rw"], "(ec p) f -> p ec f", p=128),
                      stream="w10")
        t_rb = mk.dma("gp", rb_s[:, :], inp["rbias"], stream="w10")
        out_toks = []
        zt_used = None
        wg_used = {0: None, 1: None}
        wd_used = {0: None, 1: None}
        for tc in range(2):
            ztok = []
            for ec in range(16):
                ztok.append(mk.dmat(
                    z2T[:, ec * 512:(ec + 1) * 512],
                    Z2[tc * 512:(tc + 1) * 512, ec * 128:(ec + 1) * 128],
                    waits=(zt_used,) if ec == 0 else (), stream="z2T"))
            gv = None
            for ec in range(16):
                gv = mk.op("ve", "tensor_scalar",
                           z2T[:, ec * 512:(ec + 1) * 512],
                           z2T[:, ec * 512:(ec + 1) * 512],
                           c["ln2c"][:, ec:ec + 1], None, ALU.mult,
                           waits=(ztok[ec],), inc=(ec == 15))
            with ExitStack() as st2:
                pr = st2.enter_context(nc.psum_tensor(f"pr{tc}", [128, 8], F32))
                ptr = st2.enter_context(nc.psum_tensor(f"ptr{tc}", [8, 128], BF16))
                wt_tok = None
                for t4 in range(4):
                    mmt = None
                    for ec in range(16):
                        mmt = mk.op("te", "matmul", pr[:, 0:NRE],
                                    z2T[:, ec * 512 + t4 * 128:
                                        ec * 512 + (t4 + 1) * 128],
                                    rw_s.ap()[:, ec * 8:ec * 8 + NRE],
                                    start=(ec == 0), stop=False,
                                    waits=(gv, t_rw, t_rb, wt_tok)
                                    if ec == 0 else ())
                    mmt = mk.op("te", "matmul", pr[:, 0:NRE],
                                c["ones_rowb"][:, :], rb_s[:, :],
                                start=False, stop=True, inc=True)
                    sg_ = mk.op("se", "activation", probs[:, 0:NRE],
                                pr[:, 0:NRE], AF.Sigmoid, waits=(mmt,),
                                inc=True)
                    v = mk.op("ve", "reduce_max", rcols[:, 0:1], probs[:, 0:NRE],
                              AX.X, waits=(sg_,), inc=True)
                    mk.drain("ve")
                    v = mk.op("ve", "tensor_scalar", eqs[:, 0:NRE],
                              probs[:, 0:NRE], rcols[:, 0:1], None,
                              ALU.is_equal, inc=True)
                    mk.drain("ve")
                    v = mk.op("ve", "scalar_tensor_tensor", p2s[:, 0:NRE],
                              eqs[:, 0:NRE], -1e30, probs[:, 0:NRE],
                              ALU.mult, ALU.add, inc=True)
                    mk.drain("ve")
                    v = mk.op("ve", "reduce_max", rcols[:, 1:2], p2s[:, 0:NRE],
                              AX.X, inc=True)
                    mk.drain("ve")
                    v = mk.op("ve", "tensor_scalar", p2s[:, 0:NRE],
                              p2s[:, 0:NRE], rcols[:, 1:2], None,
                              ALU.is_equal, inc=True)
                    mk.drain("ve")
                    v = mk.op("ve", "tensor_add", eqs[:, 0:NRE], eqs[:, 0:NRE],
                              p2s[:, 0:NRE], inc=True)
                    v = mk.op("ve", "tensor_add", rcols[:, 2:3], rcols[:, 0:1],
                              rcols[:, 1:2], inc=True)
                    mk.drain("ve")
                    v = mk.op("ve", "tensor_single_scalar", rcols[:, 2:3],
                              rcols[:, 2:3], 8.0, ALU.mult, inc=True)
                    mk.drain("ve")
                    v = mk.op("ve", "reciprocal", rcols[:, 3:4], rcols[:, 2:3],
                              inc=True)
                    v = mk.op("ve", "tensor_mul", wts[:, 0:NRE], probs[:, 0:NRE],
                              eqs[:, 0:NRE], inc=True)
                    mk.drain("ve")
                    v = mk.op("ve", "tensor_scalar", wts[:, 0:NRE], wts[:, 0:NRE],
                              rcols[:, 3:4], None, ALU.mult, inc=True)
                    v = mk.op("ve", "memset", wts[:, NRE:8], 0.125, inc=True)
                    mk.drain("ve")
                    v = mk.op("ve", "tensor_copy", wtb[:, :], wts[:, :], inc=True)
                    tr = mk.op("te", "transpose", ptr[:, :], wtb[:, :],
                               c["ident"][:, :], waits=(v,), inc=True)
                    wt_tok = mk.op("ve", "tensor_copy",
                                   wT[0:8, tc * 512 + t4 * 128:
                                      tc * 512 + (t4 + 1) * 128],
                                   ptr[0:8, :], waits=(tr,), inc=True)
            t_wtd = mk.dma("sy", c["WTD"][0:8, tc * 512:(tc + 1) * 512],
                           wT[0:8, tc * 512:(tc + 1) * 512], waits=(wt_tok,),
                           stream="wtd")
            wv = None
            stp2 = ExitStack()
            pwb = stp2.enter_context(nc.psum_tensor(f"pwb{tc}", [128, 512], F32))
            pg = [stp2.enter_context(nc.psum_tensor(f"pg{tc}_{i}", [128, 512], F32))
                  for i in range(4)]
            pd = [stp2.enter_context(nc.psum_tensor(f"pd{tc}_{i}", [128, 512], F32))
                  for i in range(2)]
            ma_tok = {}
            wrow_used = None
            wsb_used = None
            uv_last = None
            gb_used = None
            def gate_up(e):
                nonlocal wv, uv_last, wrow_used, wsb_used
                esl = e % 2
                gsl = e % 2
                wg_in = c["WGB"][e]
                wu_in = c["WUB"][e]
                gb_prev = ma_tok.get(("gfree", gsl))
                for ih in range(2):
                    tg = mk.dma("gp", rearr(wg_s[esl].ap(),
                                            "p (ec i) -> p ec i", ec=16),
                                rearr(wg_in[:, ih * 512:(ih + 1) * 512],
                                      "(ec p) i -> p ec i", p=128),
                                waits=(wg_used[esl], c["wcm_tok"]),
                                stream="wgu")
                    tu = mk.dma("gp", rearr(wu_s[esl].ap(),
                                            "p (ec i) -> p ec i", ec=16),
                                rearr(wu_in[:, ih * 512:(ih + 1) * 512],
                                      "(ec p) i -> p ec i", p=128),
                                waits=(wg_used[esl], c["wcm_tok"]),
                                stream="wgu")
                    if ih == 0:
                        twr = mk.dma("sy", wrow[:, :],
                                     c["WTD"][e:e + 1, tc * 512:(tc + 1) * 512],
                                     waits=(t_wtd, wrow_used), stream="wrw")
                        bt = mk.op("te", "matmul", pwb[:, :],
                                   c["ones_rowb"][:, :], wrow[:, :],
                                   start=True, stop=True,
                                   waits=(twr, wsb_used), inc=True)
                        wv = mk.op("ve", "tensor_copy", wsb[:, :], pwb[:, :],
                                   waits=(bt, uv_last), inc=True)
                        wrow_used = wv
                        wsb_used = wv
                    for ic4 in range(4):
                        ic = ih * 4 + ic4
                        pslot = ic4 % 2
                        mg = None
                        for ec in range(16):
                            mg = mk.op("te", "matmul", pg[pslot][:, :],
                                       wg_s[esl].ap()[:, ec * 512 + ic4 * 128:
                                                      ec * 512 + (ic4 + 1) * 128],
                                       z2T[:, ec * 512:(ec + 1) * 512],
                                       start=(ec == 0), stop=(ec == 15),
                                       waits=(tg, gv, wt_tok,
                                              ma_tok.get(("sg", pslot)))
                                       if ec == 0 else (), inc=(ec == 15))
                        sil = mk.op("se", "activation", sg[pslot][:, :],
                                    pg[pslot][:, :],
                                    AF.Silu, waits=(mg, ma_tok.get(("gbd", pslot))),
                                    inc=True)
                        ma_tok[("sg", pslot)] = sil
                        mu = None
                        for ec in range(16):
                            mu = mk.op("te", "matmul", pg[2 + pslot][:, :],
                                       wu_s[esl].ap()[:, ec * 512 + ic4 * 128:
                                                      ec * 512 + (ic4 + 1) * 128],
                                       z2T[:, ec * 512:(ec + 1) * 512],
                                       start=(ec == 0), stop=(ec == 15),
                                       waits=(tu, ma_tok.get(("su", pslot)))
                                       if ec == 0 else (), inc=(ec == 15))
                        uv = mk.op("ve", "tensor_tensor", su[pslot][:, :],
                                   pg[2 + pslot][:, :],
                                   wsb[:, :], ALU.mult, waits=(mu, wv),
                                   inc=True)
                        ma_tok[("su", pslot)] = uv
                        uv_last = uv
                        gb = mk.op("ve", "tensor_tensor",
                                   gbuf[gsl][:, ic * 512:(ic + 1) * 512],
                                   sg[pslot][:, :], su[pslot][:, :], ALU.mult,
                                   waits=(sil, gb_prev, uv), inc=True)
                        ma_tok[("sg", pslot)] = gb
                        ma_tok[("gbd", pslot)] = gb
                        ma_tok[("gb", gsl, ic)] = gb
                    wg_used[esl] = mu

            def down(e):
                nonlocal dn_last
                gsl = e % 2
                wd_in = c["WDB"][e]
                tw0 = mk.dma("gp", rearr(wd_s[0].ap(), "p (ic f) -> p ic f",
                                         ic=4),
                             rearr(wd_in[0:512, :], "(ic p) f -> p ic f", p=128),
                             waits=(wd_used[0], c["wcm_tok"]), stream="wd10")
                tw1 = mk.dma("gp", rearr(wd_s[1].ap(), "p (ic f) -> p ic f",
                                         ic=4),
                             rearr(wd_in[512:1024, :], "(ic p) f -> p ic f",
                                   p=128),
                             waits=(wd_used[1], c["wcm_tok"]), stream="wd10")
                md = None
                for t4 in range(4):
                    for fc in range(4):
                        ps = (t4 * 4 + fc) % 2
                        for ic in range(8):
                            md = mk.op(
                                "te", "matmul", pd[ps][:, :],
                                gbuf[gsl][:, ic * 512 + t4 * 128:
                                          ic * 512 + (t4 + 1) * 128],
                                wd_s[ic // 4].ap()[:, (ic % 4) * 2048 + fc * 512:
                                                   (ic % 4) * 2048 + (fc + 1) * 512],
                                start=(ic == 0), stop=(ic == 7),
                                waits=((tw0, ma_tok[("gb", gsl, 7)],
                                        ma_tok.get(("pd", ps)))
                                       if ic == 0 else
                                       ((tw1,) if ic == 4 else ())),
                                inc=(ic == 7))
                        dst = moeacc.ap()[:, t4 * 2048 + fc * 512:
                                          t4 * 2048 + (fc + 1) * 512]
                        prev_acc = ma_tok.get(("macc", t4, fc))
                        if e == 0:
                            mv = mk.op("ve", "tensor_copy", dst, pd[ps][:, :],
                                       waits=(md, prev_acc), inc=True)
                        else:
                            mv = mk.op("ve", "tensor_add", dst, dst,
                                       pd[ps][:, :], waits=(md, prev_acc),
                                       inc=True)
                        ma_tok[("pd", ps)] = mv
                        ma_tok[("macc", t4, fc)] = mv
                        dn_last = mv
                wd_used[0] = md
                wd_used[1] = md
                ma_tok[("gfree", gsl)] = md

            dn_last = None
            for e in range(8):
                gate_up(e)
                if e >= 1:
                    down(e - 1)
            down(7)
            for t4 in range(4):
                sl_ = t4 % 2
                tt = tc * 4 + t4
                tsp = mk.dma("sy", fin[sl_][:, :],
                             SP[tt * 128:(tt + 1) * 128, :],
                             waits=(out_toks[-1] if out_toks else None,),
                             stream="spl")
                v = mk.op("ve", "tensor_add", fin[sl_][:, :],
                          fin[sl_][:, :],
                          moeacc.ap()[:, t4 * 2048:(t4 + 1) * 2048],
                          waits=(tsp, dn_last), inc=True)
                cl = cols[sl_]
                sq = mk.op("se", "activation", c["junkb"][:, :], fin[sl_][:, :],
                           AF.Square, accum_out=cl[:, 0:1], waits=(v,), inc=True)
                v = mk.op("ve", "reduce_sum", cl[:, 1:2], fin[sl_][:, :], AX.X,
                          waits=(v,), inc=True)
                mk.drain("ve")
                v = mk.op("ve", "tensor_single_scalar", cl[:, 1:2], cl[:, 1:2],
                          1.0 / D, ALU.mult, waits=(sq,), inc=True)
                mk.drain("ve")
                v = mk.op("ve", "tensor_tensor", cl[:, 2:3], cl[:, 1:2],
                          cl[:, 1:2], ALU.mult, inc=True)
                v = mk.op("ve", "tensor_scalar", cl[:, 3:4], cl[:, 0:1],
                          1.0 / D, None, ALU.mult, inc=True)
                mk.drain("ve")
                v = mk.op("ve", "tensor_sub", cl[:, 3:4], cl[:, 3:4],
                          cl[:, 2:3], inc=True)
                sq2 = mk.op("se", "activation", cl[:, 4:5], cl[:, 3:4], AF.Sqrt,
                            bias=c["eps_col"][:, :], waits=(v,), inc=True)
                v = mk.op("ve", "reciprocal", cl[:, 5:6], cl[:, 4:5],
                          waits=(sq2,), inc=True)
                mk.drain("ve")
                v = mk.op("ve", "tensor_tensor", cl[:, 6:7], cl[:, 1:2],
                          cl[:, 5:6], ALU.mult, inc=True)
                mk.drain("ve")
                v = mk.op("ve", "tensor_single_scalar", cl[:, 6:7], cl[:, 6:7],
                          -1.0, ALU.mult, inc=True)
                mk.drain("ve")
                v = mk.op("ve", "tensor_scalar", fin[sl_][:, :], fin[sl_][:, :],
                          cl[:, 5:6], cl[:, 6:7], ALU.mult, ALU.add, inc=True)
                out_toks.append(mk.dma("sy", out[tt * 128:(tt + 1) * 128, :],
                                       fin[sl_][:, :], waits=(v,),
                                       stream="sout"))
            zt_used = wg_used[1]
            stp2.close()
        mk.wait("sy", (mk.sld, mk.sld.n))
        mk.wait("gp", (mk.gld, mk.gld.n))
        mk.flush_block()


def make_in_maps(inputs):
    f32 = lambda a: np.ascontiguousarray(np.asarray(a, dtype=np.float32))
    x = f32(inputs["L0_attn_hidden_states"])
    base = dict(
        wqkvT=f32(np.asarray(inputs["L0_attn_w_qkv"]).T),
        woutT=f32(np.asarray(inputs["L0_attn_w_out"]).T),
        ln1=f32(inputs["L0_ffn_ln1_w"]).reshape(1, D),
        ln2=f32(inputs["L0_ffn_ln2_w"]).reshape(1, D),
        kvdw=f32(inputs["L0_ffn_kv_d_w"]), qdw=f32(inputs["L0_ffn_q_d_w"]),
        kuw=f32(inputs["L0_ffn_k_u_w"]), vuw=f32(inputs["L0_ffn_v_u_w"]),
        quw=f32(inputs["L0_ffn_q_u_w"]),
        ropekw=f32(inputs["L0_ffn_rope_k_w"]),
        ropeqw=f32(inputs["L0_ffn_rope_q_w"]),
        ow=f32(inputs["L0_ffn_o_w"]),
        shg=f32(inputs["L0_ffn_sh_gate"]), shu=f32(inputs["L0_ffn_sh_up"]),
        shd=f32(inputs["L0_ffn_sh_down"]),
        rg=f32(inputs["L0_ffn_r_gate"]), ru=f32(inputs["L0_ffn_r_up"]),
        rd=f32(inputs["L0_ffn_r_down"]),
        rw=f32(inputs["L0_ffn_router_w"]),
        rbias=f32(inputs["L0_ffn_router_b"]).reshape(1, NRE),
    )
    xb = [f32(x[b]) for b in range(2)]
    xbT = [f32(x[b].T) for b in range(2)]
    maps = []
    for cix in range(8):
        b, qo = cix // 4, cix % 4
        m = dict(base)
        m["xb"] = xb[b]
        m["xbT"] = xbT[b]
        m["qstart"] = np.array([[qo * 1024.0]], np.float32)
        osel = np.zeros((1, 4), np.float32)
        osel[0, qo] = 1.0
        m["osel"] = osel
        maps.append(m)
    return maps


def kernel(**inputs):
    nc = bass.Bass("TRN2", target_bir_lowering=False)
    build(nc)
    maps = make_in_maps(inputs)
    res = run_bass_kernel_spmd(nc, maps, core_ids=list(range(8)))
    out = np.zeros((2, S, D), np.float32)
    for cix in range(8):
        b, qo = cix // 4, cix % 4
        out[b, qo * OWN:(qo + 1) * OWN] = res.results[cix]["out"]
    return out


# version 31 (final)
# speedup vs baseline: 1.0554x; 1.0554x over previous
from contextlib import ExitStack
import math
import numpy as np
import ml_dtypes
import concourse.bass as bass
import concourse.mybir as mybir
from concourse.bass_utils import run_bass_kernel_spmd

F32 = mybir.dt.float32
BF16 = mybir.dt.bfloat16
I32 = mybir.dt.int32
AF = mybir.ActivationFunctionType
ALU = mybir.AluOpType
AX = mybir.AxisListType

S = 4096
D = 2048
NH = 16
OWN = 1024
NT = S // 128
NTO = OWN // 128
FQKV = 6144
LAT = 512
IEXP = 1024
NRE = 7
EPS = 1e-5
ISC = 1.0 / math.sqrt(128.0)
ROPE_W = 1536
MROW = 5632
MOFF = 4608
V129 = NH * 129

ENGS = ("gp", "sy", "te", "ve", "se")


class Sem:
    __slots__ = ("h", "n", "name", "is_dma", "queue")

    def __init__(self, h, name, is_dma=False, queue=None):
        self.h = h
        self.n = 0
        self.name = name
        self.is_dma = is_dma
        self.queue = queue


class MK:
    def __init__(self, nc, es, taps=()):
        self.nc = nc
        self.es = es
        self.taps = set(taps)
        self.tap_specs = []
        self.plan = {k: [] for k in ENGS}
        self.wmax = {k: {} for k in ENGS}
        self.streams = {}
        self.gld = self.dsem("gp", "g0")
        self.sld = self.dsem("sy", "s0")
        self.esem = {k: self.mksem(k + "m") for k in ENGS}
        self.bar = self.mksem("bar")

    def mksem(self, name):
        return Sem(self.es.enter_context(self.nc.semaphore(name)), name)

    def dsem(self, q, name):
        key = (q, name)
        if key not in self.streams:
            self.streams[key] = Sem(
                self.es.enter_context(self.nc.semaphore(f"D{q}_{name}")),
                f"D{q}_{name}", is_dma=True, queue=q)
        return self.streams[key]

    def sbuf(self, name, shape, dtype, **kw):
        return self.es.enter_context(self.nc.sbuf_tensor(name, shape, dtype, **kw))

    def emit(self, eng, fn):
        self.plan[eng].append(fn)

    def wait(self, eng, tok):
        if tok is None:
            return
        sem, v = tok
        if sem.is_dma:
            v = sem.n
        if v <= 0 or self.wmax[eng].get(sem, 0) >= v:
            return
        self.wmax[eng][sem] = v
        self.emit(eng, lambda e, s=sem, v=v: e.wait_ge(s.h, v))

    def dma(self, q, out, in_, waits=(), stream=None, **kw):
        for t in waits:
            self.wait(q, t)
        s = self.dsem(q, stream) if stream else (
            self.gld if q == "gp" else self.sld)
        s.n += 16
        v = s.n
        self.emit(q, lambda e, o=out, i=in_, s=s, kw=kw:
                  e.dma_start(out=o, in_=i, **kw).then_inc(s.h, 16))
        return (s, v)

    def dmat(self, out, in_, waits=(), stream=None, q="sy"):
        for t in waits:
            self.wait(q, t)
        s = self.dsem(q, stream) if stream else self.sld
        s.n += 16
        v = s.n
        self.emit(q, lambda e, o=out, i=in_, s=s:
                  e.dma_start_transpose(out=o, in_=i).then_inc(s.h, 16))
        return (s, v)

    def op(self, eng, meth, *args, waits=(), inc=False, **kw):
        for t in waits:
            self.wait(eng, t)
        tok = None
        if inc:
            sm = self.esem[eng]
            sm.n += 1
            tok = (sm, sm.n)

        def fn(e, meth=meth, args=args, kw=kw, tok=tok):
            ins = getattr(e, meth)(*args, **kw)
            if tok is not None:
                ins.then_inc(tok[0].h, 1)
        self.emit(eng, fn)
        return tok

    def drain(self, eng):
        self.emit(eng, lambda e: e.drain())

    def barrier(self):
        for (q, name), s in self.streams.items():
            if name in ("wcq", "wcm"):
                continue
            self.wait(q, (s, s.n))
        b = self.bar
        target = b.n + 5
        b.n = target
        for eng in ENGS:
            if eng in ("ve", "se", "gp", "te"):
                self.drain(eng)
            self.emit(eng, lambda e, b=b: e.sem_inc(b.h, 1))
        for eng in ENGS:
            self.wait(eng, (b, target))

    def tap(self, name, src_ap, shape, dtype):
        if name in self.taps:
            o = self.nc.dram_tensor("tap_" + name, shape, dtype,
                                    kind="ExternalOutput").ap()
            self.tap_specs.append((o, src_ap))

    def flush_taps(self):
        self.barrier()
        for o, src in self.tap_specs:
            self.dma("sy", o, src, stream="tapo")
        self.wait("sy", (self.sld, self.sld.n))
        self.wait("gp", (self.gld, self.gld.n))

    def flush_block(self):
        if not any(self.plan.values()):
            return
        with self.nc.Block() as block:
            for eng, setter in (("gp", block.gpsimd), ("sy", block.sync),
                                ("te", block.tensor), ("ve", block.vector),
                                ("se", block.scalar)):
                if not self.plan[eng]:
                    continue

                def runner(e, fns=tuple(self.plan[eng])):
                    for fn in fns:
                        fn(e)
                setter(runner)
        self.plan = {k: [] for k in ENGS}


def rearr(ap, spec, **kw):
    return ap.rearrange(spec, **kw)


class Build:
    def __init__(self, nc, mk, es):
        self.nc = nc
        self.mk = mk
        self.es = es

    def sb(self, st, n, sh, dt):
        return st.enter_context(self.nc.sbuf_tensor(n, sh, dt))

    def pt(self, st, n, sh):
        return st.enter_context(self.nc.psum_tensor(n, sh, F32))


def build(nc, taps=(), phases=99):
    es = ExitStack()
    with es:
        mk = MK(nc, es, taps)
        _body(nc, mk, es, phases)
        mk.flush_block()
    return nc


def _body(nc, mk, es, phases):
    B = Build(nc, mk, es)
    inp = {}

    def ein(name, shape):
        inp[name] = nc.dram_tensor(name, shape, F32, kind="ExternalInput").ap()

    ein("xb", [S, D]); ein("xbT", [D, S])
    ein("wqkvT", [D, FQKV]); ein("woutT", [D, D])
    ein("ln1", [1, D]); ein("ln2", [1, D])
    ein("kvdw", [D, LAT]); ein("qdw", [D, LAT])
    ein("kuw", [LAT, 1024]); ein("vuw", [LAT, D]); ein("quw", [LAT, 1024])
    ein("ropekw", [D, 1024]); ein("ropeqw", [LAT, 1024])
    ein("ow", [D, D])
    ein("shg", [D, IEXP]); ein("shu", [D, IEXP]); ein("shd", [IEXP, D])
    ein("rg", [NRE, D, IEXP]); ein("ru", [NRE, D, IEXP]); ein("rd", [NRE, IEXP, D])
    ein("rw", [D, NRE]); ein("rbias", [1, NRE])
    ein("qstart", [1, 1]); ein("osel", [1, 4])
    out = nc.dram_tensor("out", [OWN, D], F32, kind="ExternalOutput").ap()

    dram = lambda n, sh, dt: nc.dram_tensor(n, sh, dt, kind="Internal").ap()
    Qo = dram("Qo", [S, D], BF16)
    Ko = dram("Ko", [S, D], BF16)
    Vo = dram("Vo", [S, V129], BF16)
    O1 = dram("O1", [S, D], BF16)
    X1 = dram("X1", [S, D], F32)
    Zd = dram("Zd", [S, D], BF16)
    KVD = dram("KVD", [S, LAT], BF16)
    K2 = dram("K2", [S, D], BF16)
    V2 = dram("V2", [S, V129], BF16)
    ZOWN = dram("ZOWN", [OWN, D], BF16)
    XO = dram("XO", [OWN, D], F32)
    QD = dram("QD", [OWN, LAT], BF16)
    Q2 = dram("Q2", [OWN, D], BF16)
    O2d = dram("O2d", [OWN, D], BF16)
    Z2 = dram("Z2", [OWN, D], BF16)
    SP = dram("SP", [OWN, D], F32)
    ROPET = dram("ROPET", [S, ROPE_W], BF16)
    WTD = dram("WTD", [8, 1024], BF16)
    WQB = dram("WQB", [D, FQKV], BF16)
    WGB = dram("WGB", [8, D, IEXP], BF16)
    WUB = dram("WUB", [8, D, IEXP], BF16)
    WDB = dram("WDB", [8, IEXP, D], BF16)
    NRMD = dram("NRMD", [64, 128], F32)

    ones_row = mk.sbuf("ones_row", [1, 128], F32)
    ones_rowb = mk.sbuf("ones_rowb", [1, 128], BF16)
    ones_colb = mk.sbuf("ones_colb", [128, 1], BF16)
    one_one = mk.sbuf("one_one", [1, 1], F32)
    iota_col = mk.sbuf("iota_col", [128, 1], F32)
    ident = mk.sbuf("ident", [128, 128], BF16)
    ln1c = mk.sbuf("ln1c", [128, 16], F32)
    ln2c = mk.sbuf("ln2c", [128, 16], F32)
    biasO = mk.sbuf("biasO", [128, 16], F32)
    bias2 = mk.sbuf("bias2", [128, 16], F32)
    qstart_c = mk.sbuf("qstart_c", [128, 1], F32)
    oselc = mk.sbuf("oselc", [128, 4], F32)
    junkb = mk.sbuf("junkb", [128, 2048], BF16)
    eps_col = mk.sbuf("eps_col", [128, 1], F32)
    nk2g = mk.sbuf("nk2g", [128, 16], F32)

    ctx = dict(locals())

    p0_consts(B, ctx)
    mk.tap("ropet", ROPET[0:S, :], [S, ROPE_W], BF16)
    if phases >= 1:
        p1_qkv(B, ctx)
        mk.tap("Qo", Qo[0:S, :], [S, D], BF16)
        mk.tap("Ko", Ko[0:S, :], [S, D], BF16)
        mk.tap("Vo", Vo[0:S, :], [S, V129], BF16)
        mk.tap("biasO", biasO[:, :], [128, 16], F32)
    if phases >= 2:
        flash(B, ctx, outer=True)
        mk.tap("O1", O1[0:S, :], [S, D], BF16)
    if phases >= 3:
        p3_outproj(B, ctx)
        mk.tap("X1", X1[0:S, :], [S, D], F32)
        mk.tap("Zd", Zd[0:S, :], [S, D], BF16)
    if phases >= 4:
        p5_mlakv(B, ctx)
        mk.tap("K2", K2[0:S, :], [S, D], BF16)
        mk.tap("V2", V2[0:S, :], [S, V129], BF16)
        mk.tap("KVD", KVD[0:S, :], [S, LAT], BF16)
    if phases >= 5:
        p7_qside(B, ctx)
        mk.tap("Q2", Q2[0:OWN, :], [OWN, D], BF16)
        mk.tap("bias2", bias2[:, :], [128, 16], F32)
        mk.tap("ZOWN", ZOWN[0:OWN, :], [OWN, D], BF16)
    if phases >= 6:
        flash(B, ctx, outer=False)
        mk.tap("O2d", O2d[0:OWN, :], [OWN, D], BF16)
    if phases >= 7:
        p9_ow(B, ctx)
        mk.tap("Z2", Z2[0:OWN, :], [OWN, D], BF16)
        mk.tap("SP", SP[0:OWN, :], [OWN, D], F32)
    if phases >= 8:
        p10_moe(B, ctx)
        mk.flush_taps()
        return
    mk.flush_taps()
    dummy_out(mk, out)


def dummy_out(mk, out):
    z = mk.sbuf("zz_dummy", [128, D], F32)
    t = mk.op("ve", "memset", z[:, :], 0.0, inc=True)
    last = None
    for i in range(OWN // 128):
        last = mk.dma("sy", out[i * 128:(i + 1) * 128, :], z[:, :], waits=(t,), stream="tapo")
    mk.wait("sy", last)
    mk.wait("gp", (mk.gld, mk.gld.n))


def p0_consts(B, c):
    nc, mk = B.nc, B.mk
    inp = c["inp"]
    with ExitStack() as st:
        sb = lambda n, sh, dt: B.sb(st, n, sh, dt)
        WQB, WGB, WUB, WDB = c["WQB"], c["WGB"], c["WUB"], c["WDB"]
        for piece in range(4):
            sl = slice(piece * (FQKV // 4), (piece + 1) * (FQKV // 4))
            mk.dma("gp", WQB[:, sl], inp["wqkvT"][:, sl], stream="wcq")
        for e in range(8):
            mk.dma("gp", WGB[e], inp["rg"][e] if e < NRE else inp["shg"],
                   stream="wcm")
            mk.dma("gp", WUB[e], inp["ru"][e] if e < NRE else inp["shu"],
                   stream="wcm")
            mk.dma("gp", WDB[e], inp["rd"][e] if e < NRE else inp["shd"],
                   stream="wcm")
        c["wcq_tok"] = (mk.streams[("gp", "wcq")], mk.streams[("gp", "wcq")].n)
        c["wcm_tok"] = (mk.streams[("gp", "wcm")], mk.streams[("gp", "wcm")].n)
        mk.op("gp", "memset", c["ones_row"][:, :], 1.0)
        mk.op("gp", "memset", c["ones_rowb"][:, :], 1.0)
        mk.op("gp", "memset", c["ones_colb"][:, :], 1.0)
        mk.op("gp", "memset", c["one_one"][:, :], 1.0)
        mk.op("gp", "memset", c["eps_col"][:, :], EPS)
        g1 = mk.op("gp", "iota", c["iota_col"][:, :], [[1, 1]],
                   channel_multiplier=1, allow_small_or_imprecise_dtypes=True,
                   inc=True)
        irow = sb("irow", [128, 128], F32)
        g2 = mk.op("gp", "iota", irow[:, :], [[1, 128]], channel_multiplier=0,
                   allow_small_or_imprecise_dtypes=True, inc=True)
        v = mk.op("ve", "tensor_scalar", irow[:, :], irow[:, :],
                  c["iota_col"][:, :], None, ALU.subtract, waits=(g1, g2),
                  inc=True)
        mk.drain("ve")
        mk.op("ve", "tensor_single_scalar", c["ident"][:, :], irow[:, :],
              0.0, ALU.is_equal, inc=True)

        qs_sb = sb("qs_sb", [1, 1], F32)
        os_sb = sb("os_sb", [1, 4], F32)
        psb = B.pt(st, "psb", [128, 1])
        tq = mk.dma("gp", qs_sb[:, :], inp["qstart"], stream="p0ld")
        to = mk.dma("gp", os_sb[:, :], inp["osel"], stream="p0ld")
        t1 = mk.op("te", "matmul", psb[:, :], c["ones_row"][:, :], qs_sb[:, :],
                   start=True, stop=True, waits=(tq,), inc=True)
        pv = mk.op("ve", "tensor_copy", c["qstart_c"][:, :], psb[:, :],
                   waits=(t1,), inc=True)
        for q in range(4):
            t1 = mk.op("te", "matmul", psb[:, :], c["ones_row"][:, :],
                       os_sb[0:1, q:q + 1], start=True, stop=True,
                       waits=(to, pv), inc=True)
            pv = mk.op("ve", "tensor_copy", c["oselc"][:, q:q + 1], psb[:, :],
                       waits=(t1,), inc=True)
        lnrow = sb("lnrow", [1, D], F32)
        for name, dst in (("ln1", c["ln1c"]), ("ln2", c["ln2c"])):
            t = mk.dma("gp", lnrow[:, :], inp[name], waits=(pv,), stream="p0ld")
            for ec in range(16):
                t1 = mk.op("te", "matmul", psb[:, :],
                           lnrow[0:1, ec * 128:(ec + 1) * 128], c["one_one"][:, :],
                           start=True, stop=True, waits=(t, pv), inc=True)
                pv = mk.op("ve", "tensor_copy", dst[:, ec:ec + 1], psb[:, :],
                           waits=(t1,), inc=True)

        invf = sb("invf", [1, 32], F32)
        coefrow = sb("coefrow", [1, ROPE_W], F32)
        shiftrow = sb("shiftrow", [1, ROPE_W], F32)
        g3 = mk.op("gp", "iota", invf[:, :], [[1, 32]], channel_multiplier=0,
                   allow_small_or_imprecise_dtypes=True, inc=True)
        stt = mk.op("se", "activation", invf[:, :], invf[:, :], AF.Exp,
                    scale=-math.log(10000.0) / 32.0, waits=(g3,), inc=True)
        cm = 1.0 / (2.0 * math.pi)
        cr4 = rearr(coefrow.ap()[0:1, 0:512], "o (a i t) -> o a i t", a=8, t=2)
        iv4 = rearr(invf.ap(), "o (a i t) -> o a i t", a=1, t=1)
        v = None
        for a in range(8):
            for tp in range(2):
                v = mk.op("ve", "tensor_scalar", cr4[:, a:a + 1, :, tp:tp + 1],
                          iv4, cm, None, ALU.mult, waits=(stt,), inc=True)
        cr8 = rearr(coefrow.ap()[0:1, 512:1536], "o (a j) -> o a j", j=32)
        iv3 = rearr(invf.ap(), "o (a j) -> o a j", a=1)
        for blk in range(32):
            v = mk.op("ve", "tensor_scalar", cr8[:, blk:blk + 1, :], iv3, cm,
                      None, ALU.mult, inc=True)
        v = mk.op("ve", "memset", shiftrow[:, :], 0.0, inc=True)
        mk.drain("ve")
        v = mk.op("ve", "memset", shiftrow[0:1, 0:256], 0.25, inc=True)
        sh4 = rearr(shiftrow.ap()[0:1, 256:512], "o (i t) -> o i t", t=2)
        v = mk.op("ve", "memset", sh4[:, :, 0:1], 0.5, inc=True)
        v = mk.op("ve", "memset", shiftrow[0:1, 512:1024], 0.25, inc=True)
        sh8 = rearr(shiftrow.ap()[0:1, 1024:1536], "o (a j) -> o a j", j=32)
        for blk in range(0, 16, 2):
            v = mk.op("ve", "memset", sh8[:, blk:blk + 1, :], 0.5, inc=True)
        coef_tok = v
        mk.drain("ve")

        tposr = sb("tposr", [1, 128], F32)
        angi = sb("angi", [128, ROPE_W], I32)
        angf = sb("angf", [128, ROPE_W], F32)
        angc = sb("angc", [128, ROPE_W], F32)
        rtab = sb("rtab", [128, ROPE_W], BF16)
        psa = st.enter_context(nc.psum_tensor("psa", [128, ROPE_W], F32))
        g4 = mk.op("gp", "iota", tposr[:, :], [[1, 128]], channel_multiplier=0,
                   allow_small_or_imprecise_dtypes=True, inc=True)
        prev_sin = None
        prev_store = None
        ROPET = c["ROPET"]
        for tt in range(NT):
            if tt:
                v = mk.op("ve", "tensor_single_scalar", tposr[:, :], tposr[:, :],
                          128.0, ALU.add, waits=(g4, coef_tok, prev_sin),
                          inc=True)
                mk.drain("ve")
            else:
                v = coef_tok
                mk.wait("te", g4)
            mmt = None
            for seg in range(3):
                sl = psa.ap()[:, seg * 512:(seg + 1) * 512]
                mk.op("te", "matmul", sl, tposr[:, :],
                      coefrow.ap()[0:1, seg * 512:(seg + 1) * 512],
                      start=True, stop=False, waits=(v, prev_sin))
                mmt = mk.op("te", "matmul", sl, c["ones_row"][:, :],
                            shiftrow.ap()[0:1, seg * 512:(seg + 1) * 512],
                            start=False, stop=True, inc=(seg == 2))
            v1 = mk.op("ve", "tensor_copy", angi[:, :], psa[:, :],
                       waits=(mmt, prev_store), inc=True)
            mk.drain("ve")
            v2 = mk.op("ve", "tensor_copy", angf[:, :], angi[:, :], inc=True)
            mk.drain("ve")
            v3 = mk.op("ve", "tensor_sub", angf[:, :], psa[:, :], angf[:, :],
                       inc=True)
            mk.drain("ve")
            v3 = mk.op("ve", "tensor_single_scalar", angc[:, :], angf[:, :],
                       0.5, ALU.is_gt, inc=True)
            mk.drain("ve")
            v3 = mk.op("ve", "tensor_sub", angf[:, :], angf[:, :], angc[:, :],
                       inc=True)
            prev_sin = mk.op("se", "activation", rtab[:, :], angf[:, :], AF.Sin,
                             scale=2.0 * math.pi, waits=(v3,), inc=True)
            prev_store = mk.dma("sy", ROPET[tt * 128:(tt + 1) * 128, :],
                                rtab[:, :], waits=(prev_sin,), stream="p0rt")
        mk.barrier()
        mk.flush_block()


def p1_qkv(B, c):
    nc, mk = B.nc, B.mk
    inp = c["inp"]
    Qo, Ko, Vo, ROPET = c["Qo"], c["Ko"], c["Vo"], c["ROPET"]
    with ExitStack() as st:
        sb = lambda n, sh, dt: B.sb(st, n, sh, dt)
        xts = [sb(f"xts{i}", [128, 16 * 1024], BF16) for i in range(2)]
        rtabs = [sb(f"rtabs{i}", [128, 8 * ROPE_W], BF16) for i in range(2)]
        wsl = [sb(f"wsl{i}", [128, 16 * 512], BF16) for i in range(2)]
        cstage = [sb(f"cst{i}", [128, 512], F32) for i in range(2)]
        rt1 = sb("rt1", [128, 256], F32)
        rt2 = sb("rt2", [128, 256], F32)
        qkst = [sb(f"qkst{i}", [128, 512], BF16) for i in range(3)]
        vst = [sb(f"vst{i}", [128, 516], BF16) for i in range(3)]
        nq = sb("nq", [128, 16], F32)
        nk = sb("nk", [128, 16], F32)
        nqt = sb("nqt", [128, 4], F32)
        pp = [st.enter_context(nc.psum_tensor(f"pp{i}", [128, 512], F32))
              for i in range(2)]

        z0 = mk.op("ve", "memset", nq[:, :], 0.0, inc=True)
        z0 = mk.op("ve", "memset", nk[:, :], 0.0, inc=True)
        vz = [mk.op("ve", "memset", vst[i][:, :], 1.0, inc=True)
              for i in range(3)]

        ev_tok = {0: None, 1: None}
        qk_st = {0: vz[0], 1: vz[1], 2: vz[2]}
        v_st = {0: vz[0], 1: vz[1], 2: vz[2]}
        wl_gate = None
        np_used = {0: None, 1: None}
        nacc_tok = {0: z0, 2: z0}

        def issue_qloads(qr, gate):
            qsl = qr % 2
            qoff = qr * 1024
            xt = mk.dma(
                "gp", rearr(xts[qsl].ap(), "p (ec t) -> p ec t", ec=16),
                rearr(inp["xbT"][:, qoff:qoff + 1024],
                      "(ec p) t -> p ec t", p=128),
                waits=(gate,), stream="xts")
            rt = mk.dma(
                "sy", rearr(rtabs[qsl].ap(), "p (tt w) -> p tt w", tt=8),
                rearr(ROPET[qoff:qoff + 1024, :], "(tt p) w -> p tt w", p=128),
                waits=(gate,), stream="rtb1")
            return (xt, rt)

        qload_tok = {0: issue_qloads(0, None)}
        for qr in range(4):
            qsl = qr % 2
            xt_tok, rt_tok = qload_tok[qr]
            first_q_mm = None
            for fc in range(12):
                slot = fc % 2
                wl_tok = mk.dma(
                    "gp", rearr(wsl[slot].ap(), "p (ec f) -> p ec f", ec=16),
                    rearr(c["WQB"][:, fc * 512:(fc + 1) * 512],
                          "(ec p) f -> p ec f", p=128),
                    waits=(wl_gate, c["wcq_tok"]), stream="wq")
                g, kind = divmod(fc, 3)
                for tt in range(8):
                    gtt = qr * 8 + tt
                    pslot = tt % 2
                    w0 = [wl_tok, ev_tok[pslot], xt_tok]
                    mmtok = None
                    for ec in range(16):
                        mmtok = mk.op(
                            "te", "matmul", pp[pslot][:, :],
                            xts[qsl].ap()[:, ec * 1024 + tt * 128:
                                          ec * 1024 + (tt + 1) * 128],
                            wsl[slot].ap()[:, ec * 512:(ec + 1) * 512],
                            start=(ec == 0), stop=(ec == 15),
                            waits=w0 if ec == 0 else (),
                            inc=(ec == 15 or (ec == 0 and tt == 0)))
                    if tt == 0:
                        wl_gate = (mk.esem["te"], mk.esem["te"].n - 1)
                        if fc == 0:
                            first_q_mm = wl_gate
                    if kind == 1:
                        vslot = tt % 3
                        ev = mk.op(
                            "ve", "tensor_copy",
                            rearr(vst[vslot].ap(), "p (h x) -> p h x", h=4)
                            [:, :, 0:128],
                            rearr(pp[pslot].ap(), "p (h x) -> p h x", h=4),
                            waits=(mmtok, v_st[vslot]), inc=True)
                        ev_tok[pslot] = ev
                        v_st[vslot] = mk.dma(
                            "sy", Vo[gtt * 128:(gtt + 1) * 128,
                                     g * 516:(g + 1) * 516],
                            vst[vslot][:, :], waits=(ev,), stream="stv")
                    else:
                        ev = mk.op("ve", "tensor_copy", cstage[pslot][:, :],
                                   pp[pslot][:, :], waits=(mmtok,), inc=True)
                        sq = mk.op("se", "activation", c["junkb"][:, 0:512],
                                   pp[pslot][:, :], AF.Square,
                                   waits=(mmtok, ev, np_used[0]), inc=True)
                        ev_tok[pslot] = sq
                        accdst = nq if kind == 0 else nk
                        for hh in range(4):
                            mk.op("ve", "reduce_sum", nqt[:, hh:hh + 1],
                                  c["junkb"][:, hh * 128:(hh + 1) * 128], AX.X,
                                  waits=(sq,) if hh == 0 else ())
                        mk.drain("ve")
                        na = mk.op("ve", "tensor_max",
                                   accdst[:, g * 4:(g + 1) * 4],
                                   accdst[:, g * 4:(g + 1) * 4], nqt[:, :],
                                   waits=(nacc_tok[kind],), inc=True)
                        nacc_tok[kind] = na
                        np_used[0] = na
                        oslot = (fc + tt) % 3
                        cs4 = rearr(cstage[pslot].ap(),
                                    "p (h i t) -> p h i t", h=4, t=2)
                        base = tt * ROPE_W
                        C4 = rearr(rtabs[qsl].ap()[:, base:base + 256],
                                   "p (h i t) -> p h i t", h=4, t=2)
                        S4 = rearr(rtabs[qsl].ap()[:, base + 256:base + 512],
                                   "p (h i t) -> p h i t", h=4, t=2)
                        r1 = rearr(rt1.ap(), "p (h i t) -> p h i t", h=4, t=2)
                        r2 = rearr(rt2.ap(), "p (h i t) -> p h i t", h=4, t=2)
                        o4 = rearr(qkst[oslot].ap(),
                                   "p (h i t) -> p h i t", h=4, t=2)
                        v = mk.op("ve", "tensor_tensor", r1[:, :, 0:32, :],
                                  cs4[:, :, 0:32, :], C4[:, :, 0:32, :],
                                  ALU.mult, waits=(ev, rt_tok, qk_st[oslot]),
                                  inc=True)
                        v = mk.op("ve", "tensor_tensor", r2[:, :, 0:32, 0:1],
                                  cs4[:, :, 0:32, 1:2], S4[:, :, 0:32, 0:1],
                                  ALU.mult, inc=True)
                        v = mk.op("ve", "tensor_tensor", r2[:, :, 0:32, 1:2],
                                  cs4[:, :, 0:32, 0:1], S4[:, :, 0:32, 1:2],
                                  ALU.mult, inc=True)
                        mk.drain("ve")
                        v = mk.op("ve", "tensor_add", o4[:, :, 0:32, :],
                                  r1[:, :, 0:32, :], r2[:, :, 0:32, :], inc=True)
                        v = mk.op("ve", "tensor_copy", o4[:, :, 32:64, :],
                                  cs4[:, :, 32:64, :], inc=True)
                        dstD = Qo if kind == 0 else Ko
                        qk_st[oslot] = mk.dma(
                            "sy", dstD[gtt * 128:(gtt + 1) * 128,
                                       g * 512:(g + 1) * 512],
                            qkst[oslot][:, :], waits=(v,), stream="stqk")
                if fc == 0 and qr + 1 < 4:
                    qload_tok[qr + 1] = issue_qloads(qr + 1, first_q_mm)
        finalize_bound(B, c, st, nq, nk, nacc_tok[0], nacc_tok[2],
                       c["biasO"], rowbase=0)
        mk.barrier()
        mk.flush_block()


def finalize_bound(B, c, st, nqacc, nkacc, tok_q, tok_k, bias_dst, rowbase):
    nc, mk = B.nc, B.mk
    NRMD = c["NRMD"]
    rows = B.sb(st, f"rows{rowbase}", [16, 128], F32)
    qcol = B.sb(st, f"qcol{rowbase}", [16, 2], F32)
    mrow = B.sb(st, f"mrow{rowbase}", [1, 16], F32)
    psb = st.enter_context(nc.psum_tensor(f"psbb{rowbase}", [128, 16], F32))
    toks = []
    for h in range(16):
        toks.append(mk.dma("sy", rearr(NRMD[rowbase + h:rowbase + h + 1, :],
                                       "o p -> p o"),
                           nqacc[:, h:h + 1], waits=(tok_q,), stream="nrm"))
        toks.append(mk.dma("sy", rearr(NRMD[rowbase + 16 + h:rowbase + 17 + h, :],
                                       "o p -> p o"),
                           nkacc[:, h:h + 1], waits=(tok_k,), stream="nrm"))
    t1 = mk.dma("sy", rows[:, :], NRMD[rowbase:rowbase + 16, :], waits=toks, stream="nrm")
    v = mk.op("ve", "reduce_max", qcol[:, 0:1], rows[:, :], AX.X,
              waits=(t1,), inc=True)
    t2 = mk.dma("sy", rows[:, :], NRMD[rowbase + 16:rowbase + 32, :],
                waits=(t1, v), stream="nrm")
    v = mk.op("ve", "reduce_max", qcol[:, 1:2], rows[:, :], AX.X,
              waits=(t2,), inc=True)
    mk.drain("ve")
    v = mk.op("ve", "tensor_add", qcol[:, 0:1], qcol[:, 0:1], qcol[:, 1:2],
              inc=True)
    mk.drain("ve")
    v = mk.op("ve", "tensor_single_scalar", qcol[:, 0:1], qcol[:, 0:1],
              -1.02 / (2.0 * math.sqrt(128.0)), ALU.mult, inc=True)
    t3 = mk.dma("sy", rearr(NRMD[rowbase:rowbase + 1, 0:16], "o p -> p o"),
                qcol[0:16, 0:1], waits=(v,), stream="nrm")
    t4 = mk.dma("sy", mrow[:, :], NRMD[rowbase:rowbase + 1, 0:16], waits=(t3,), stream="nrm")
    t5 = mk.op("te", "matmul", psb[:, :], c["ones_row"][:, :], mrow[:, :],
               start=True, stop=True, waits=(t4,), inc=True)
    mk.op("ve", "tensor_copy", bias_dst[:, :], psb[:, :], waits=(t5,), inc=True)


def flash(B, c, outer):
    nc, mk = B.nc, B.mk
    if outer:
        Qsrc, Ksrc, Vsrc, Odst = c["Qo"], c["Ko"], c["Vo"], c["O1"]
        bias, SQ, nqc = c["biasO"], S, 8
    else:
        Qsrc, Ksrc, Vsrc, Odst = c["Q2"], c["K2"], c["V2"], c["O2d"]
        bias, SQ, nqc = c["bias2"], OWN, 2
    tag = "o" if outer else "i"
    with ExitStack() as st:
        sb = lambda n, sh, dt: B.sb(st, n + tag, sh, dt)
        iorow = sb("iorow", [128, MROW], F32)
        master = sb("master", [128, MROW], BF16)
        g1 = mk.op("gp", "iota", iorow[:, :], [[1, MROW]], channel_multiplier=0,
                   allow_small_or_imprecise_dtypes=True, inc=True)
        v = mk.op("ve", "tensor_scalar", iorow[:, :], iorow[:, :],
                  c["iota_col"][:, :], None, ALU.subtract, waits=(g1,), inc=True)
        mk.drain("ve")
        if outer:
            mtok = mk.op("ve", "tensor_single_scalar", master[:, :], iorow[:, :],
                         float(MOFF), ALU.is_ge, inc=True)
        else:
            thr = sb("thr", [128, 1], F32)
            v = mk.op("ve", "tensor_scalar", thr[:, :], c["qstart_c"][:, :],
                      -1.0, float(MOFF), ALU.mult, ALU.add, inc=True)
            mk.drain("ve")
            mtok = mk.op("ve", "tensor_scalar", master[:, :], iorow[:, :],
                         thr[:, :], None, ALU.is_ge, inc=True)

        KhT = [sb(f"KhT{i}", [128, S], BF16) for i in range(2)]
        QhT = [sb(f"QhT{i}", [128, SQ], BF16) for i in range(2)]
        Vh = [sb(f"Vh{i}", [128, 32 * 129], BF16) for i in range(2)]
        PT = [sb(f"PT{i}", [128, 1024], BF16) for i in range(3)]
        fst = [sb(f"fst{i}", [128, 512], BF16) for i in range(2)]
        linv = sb("linv", [128, 4], F32)
        stp = [st.enter_context(nc.psum_tensor(f"stp{tag}{i}", [128, 1024], F32))
               for i in range(2)]
        otp = [st.enter_context(nc.psum_tensor(f"otp{tag}{i}", [128, 258], F32))
               for i in range(4)]

        load_tok = {}
        exp_tok = {0: None, 1: None}
        ptfree = {0: None, 1: None, 2: None}
        ot_free = {0: None, 1: None}
        fs_tok = {0: None, 1: None}

        def issue_loads(h, gate=None):
            hb = h % 2
            kt_ = mk.dmat(KhT[hb][:, :], Ksrc[:, h * 128:(h + 1) * 128],
                          waits=(gate,), stream="fkq")
            qt_ = mk.dmat(QhT[hb][:, :], Qsrc[:, h * 128:(h + 1) * 128],
                          stream="fkq")
            vt_ = mk.dma("sy", rearr(Vh[hb].ap(), "p (kt v) -> p kt v", kt=32),
                         rearr(Vsrc[:, h * 129:(h + 1) * 129],
                               "(kt p) v -> p kt v", p=128), stream="fv")
            load_tok[h] = (kt_, qt_, vt_)

        issue_loads(0)
        for h in range(NH):
            hb = h % 2
            if h + 1 < NH:
                pass
            ktok, qtok, vtok = load_tok[h]
            last_pv = None
            first_pv1 = None
            for qc in range(nqc):
                nkt = 4 * (qc + 1) if outer else 32
                npairs = nkt // 2
                def emit_pv(pp_, ptok_):
                    nonlocal last_pv, first_pv1
                    ss_ = pp_ % 3
                    ob_ = 2 * (qc % 2)
                    pv = None
                    for half in range(2):
                        kt = 2 * pp_ + half
                        for qs in range(4):
                            oi, col = qs // 2, (qs % 2) * 129
                            first = (pp_ == 0 and qc == 0 and half == 0
                                     and qs == 0)
                            pre_n2 = mk.esem["te"].n
                            pv = mk.op(
                                "te", "matmul", otp[ob_ + oi][:, col:col + 129],
                                PT[ss_][:, half * 512 + qs * 128:
                                        half * 512 + (qs + 1) * 128],
                                Vh[hb][:, kt * 129:(kt + 1) * 129],
                                start=(pp_ == 0 and half == 0 and qs % 2 == 0),
                                stop=(pp_ == npairs - 1 and half == 1
                                      and qs % 2 == 1),
                                skip_group_check=True,
                                waits=(ptok_, vtok, ot_free[qc % 2])
                                if (qs == 0 and half == 0) else (),
                                inc=((half == 1 and qs == 3) or first))
                            if first:
                                first_pv1 = (mk.esem["te"], pre_n2 + 1)
                    ptfree[ss_] = pv
                    last_pv = pv

                ptoks = {}
                for p in range(npairs):
                    ss = p % 3
                    ssq = p % 2
                    mm1 = None
                    for half in range(2):
                        kt = 2 * p + half
                        pre_n = mk.esem["te"].n
                        mm1 = mk.op(
                            "te", "matmul", stp[ssq][:, half * 512:(half + 1) * 512],
                            KhT[hb][:, kt * 128:(kt + 1) * 128],
                            QhT[hb][:, qc * 512:(qc + 1) * 512],
                            start=True, stop=True,
                            waits=(exp_tok[ssq], ktok, qtok, mtok) if half == 0
                            else (),
                            inc=(half == 1 or (p == 0 and qc == 0
                                               and half == 0)))
                        if p == 0 and qc == 0 and half == 0:
                            first_mm1 = (mk.esem["te"], pre_n + 1)
                    ex = mk.op("se", "activation", PT[ss][:, :], stp[ssq][:, :],
                               AF.Exp, bias=bias[:, h:h + 1], scale=ISC,
                               waits=(mm1, ptfree[ss]), inc=True)
                    exp_tok[ssq] = ex
                    ptok = ex
                    for half in range(2):
                        kt = 2 * p + half
                        if outer:
                            dj = kt - 4 * qc
                            masked = 0 <= dj < 4
                            off = MOFF - dj * 128
                        else:
                            masked = True
                            off = qc * 512 - kt * 128 + MOFF
                        if masked:
                            ptok = mk.op(
                                "ve", "tensor_tensor",
                                PT[ss][:, half * 512:(half + 1) * 512],
                                PT[ss][:, half * 512:(half + 1) * 512],
                                master[:, off:off + 512], ALU.mult,
                                waits=(ex,), inc=True)
                    ptoks[p] = ptok
                    if p >= 1:
                        emit_pv(p - 1, ptoks.pop(p - 1))
                emit_pv(npairs - 1, ptoks.pop(npairs - 1))
                if qc == 0 and h + 1 < NH:
                    issue_loads(h + 1, gate=first_pv1)
                fs = qc % 2
                for qs in range(4):
                    oi, col = qs // 2, (qs % 2) * 129
                    mk.op("ve", "reciprocal", linv[:, qs:qs + 1],
                          otp[2 * fs + oi][:, col + 128:col + 129],
                          waits=(last_pv, fs_tok[fs]) if qs == 0 else ())
                mk.drain("ve")
                ev = None
                for qs in range(4):
                    oi, col = qs // 2, (qs % 2) * 129
                    ev = mk.op("ve", "tensor_scalar",
                               fst[fs][:, qs * 128:(qs + 1) * 128],
                               otp[2 * fs + oi][:, col:col + 128],
                               linv[:, qs:qs + 1],
                               None, ALU.mult, inc=(qs == 3))
                ot_free[fs] = ev
                fs_tok[fs] = mk.dma(
                    "gp", rearr(Odst[qc * 512:(qc + 1) * 512,
                                     h * 128:(h + 1) * 128],
                                "(qs p) d -> p qs d", p=128),
                    rearr(fst[fs].ap(), "p (qs d) -> p qs d", qs=4),
                    waits=(ev,), stream="fo")

        mk.barrier()
        mk.flush_block()



def p3_outproj(B, c):
    nc, mk = B.nc, B.mk
    inp, O1, X1, Zd = c["inp"], c["O1"], c["X1"], c["Zd"]
    with ExitStack() as st:
        sb = lambda n, sh, dt: B.sb(st, n, sh, dt)
        wsl = sb("wslo", [128, 16 * 2048], BF16)
        o1t = [sb(f"o1t{i}", [128, 16 * 512], BF16) for i in range(2)]
        xbt = [sb(f"xbt{i}", [128, 2048], F32) for i in range(2)]
        x1s = [sb(f"x1s{i}", [128, 2048], F32) for i in range(2)]
        zs = [sb(f"zs{i}", [128, 2048], BF16) for i in range(2)]
        cols = [sb(f"cols{i}", [128, 8], F32) for i in range(2)]
        pp = [st.enter_context(nc.psum_tensor(f"pp3{i}", [128, 512], F32))
              for i in range(2)]
        wtok = []
        for piece in range(4):
            sl = slice(piece * 4, (piece + 1) * 4)
            wtok.append(mk.dma(
                "gp", rearr(wsl.ap(), "p (dc f) -> p dc f", dc=16)[:, sl, :],
                rearr(inp["woutT"], "(dc p) f -> p dc f", p=128)[:, sl, :],
                stream="wo3"))
        o1_used = {0: None, 1: None}
        ev_done = {0: None, 1: None}
        x1_used = {0: (), 1: ()}
        for tg in range(8):
            sl_ = tg % 2
            ot = []
            for dc in range(16):
                ot.append(mk.dmat(o1t[sl_][:, dc * 512:(dc + 1) * 512],
                                  O1[tg * 512:(tg + 1) * 512,
                                     dc * 128:(dc + 1) * 128],
                                  waits=(o1_used[sl_],) if dc == 0 else (),
                                  stream="o1t"))
            last_mm = None
            for t4 in range(4):
                tt = tg * 4 + t4
                xt = mk.dma("sy", xbt[t4 % 2][:, :],
                            inp["xb"][tt * 128:(tt + 1) * 128, :],
                            waits=x1_used[t4 % 2] or (), stream="xb3")
                evl = None
                mmt = None
                for fc in range(4):
                    pslot = fc % 2
                    for dc in range(16):
                        mmt = mk.op(
                            "te", "matmul", pp[pslot][:, :],
                            o1t[sl_][:, dc * 512 + t4 * 128:
                                     dc * 512 + (t4 + 1) * 128],
                            wsl.ap()[:, dc * 2048 + fc * 512:
                                     dc * 2048 + (fc + 1) * 512],
                            start=(dc == 0), stop=(dc == 15),
                            waits=tuple(ot) + tuple(wtok) + (ev_done[pslot],)
                            if dc == 0 else (), inc=(dc == 15))
                    evl = mk.op("ve", "scalar_tensor_tensor",
                                x1s[t4 % 2][:, fc * 512:(fc + 1) * 512],
                                pp[pslot][:, :], 0.0,
                                xbt[t4 % 2][:, fc * 512:(fc + 1) * 512],
                                ALU.bypass, ALU.add, waits=(mmt, xt), inc=True)
                    ev_done[pslot] = evl
                last_mm = mmt
                x1_used[t4 % 2] = ln_z_epilogue(
                    B, c, x1s[t4 % 2], zs[t4 % 2], cols[t4 % 2], evl,
                    X1[tt * 128:(tt + 1) * 128, :],
                    Zd[tt * 128:(tt + 1) * 128, :])
            o1_used[sl_] = last_mm
        mk.barrier()
        mk.flush_block()


def ln_z_epilogue(B, c, x1s, zs, cl, evl, Xrow, Zrow):
    nc, mk = B.nc, B.mk
    sq = mk.op("se", "activation", c["junkb"][:, :], x1s[:, :],
               AF.Square, accum_out=cl[:, 0:1], waits=(evl,), inc=True)
    v = mk.op("ve", "reduce_sum", cl[:, 1:2], x1s[:, :], AX.X,
              waits=(evl,), inc=True)
    mk.drain("ve")
    v = mk.op("ve", "tensor_single_scalar", cl[:, 1:2], cl[:, 1:2],
              1.0 / D, ALU.mult, waits=(sq,), inc=True)
    mk.drain("ve")
    v = mk.op("ve", "tensor_tensor", cl[:, 2:3], cl[:, 1:2], cl[:, 1:2],
              ALU.mult, inc=True)
    v = mk.op("ve", "tensor_scalar", cl[:, 3:4], cl[:, 0:1], 1.0 / D,
              None, ALU.mult, inc=True)
    mk.drain("ve")
    v = mk.op("ve", "tensor_sub", cl[:, 3:4], cl[:, 3:4], cl[:, 2:3],
              inc=True)
    sqv = mk.op("se", "activation", cl[:, 4:5], cl[:, 3:4], AF.Sqrt,
                bias=c["eps_col"][:, :], waits=(v,), inc=True)
    v = mk.op("ve", "reciprocal", cl[:, 5:6], cl[:, 4:5],
              waits=(sqv,), inc=True)
    mk.drain("ve")
    v = mk.op("ve", "tensor_tensor", cl[:, 6:7], cl[:, 1:2], cl[:, 5:6],
              ALU.mult, inc=True)
    mk.drain("ve")
    v = mk.op("ve", "tensor_single_scalar", cl[:, 6:7], cl[:, 6:7],
              -1.0, ALU.mult, inc=True)
    mk.drain("ve")
    v = mk.op("ve", "tensor_scalar", x1s[:, :], x1s[:, :],
              cl[:, 5:6], cl[:, 6:7], ALU.mult, ALU.add, inc=True)
    tx = mk.dma("gp", Xrow, x1s[:, :], waits=(v,), stream="sx1")
    v2 = mk.op("ve", "tensor_tensor", cl[:, 7:8], cl[:, 3:4], cl[:, 5:6],
               ALU.mult, waits=(v,), inc=True)
    mk.drain("ve")
    v2 = mk.op("ve", "tensor_tensor", cl[:, 7:8], cl[:, 7:8], cl[:, 5:6],
               ALU.mult, inc=True)
    sqv = mk.op("se", "activation", cl[:, 7:8], cl[:, 7:8], AF.Sqrt,
                bias=c["eps_col"][:, :], waits=(v2,), inc=True)
    v2 = mk.op("ve", "reciprocal", cl[:, 7:8], cl[:, 7:8],
               waits=(sqv,), inc=True)
    mk.drain("ve")
    v2 = mk.op("ve", "tensor_scalar", zs[:, :], x1s[:, :],
               cl[:, 7:8], None, ALU.mult, inc=True)
    tz = mk.dma("gp", Zrow, zs[:, :], waits=(v2,), stream="szd")
    return (tx, tz, v2)


def p6_load(B, c, g6, tto):
    mk = B.mk
    Zd, X1 = c["Zd"], c["X1"]
    tz = mk.dma("gp", rearr(g6["zg"].ap(), "p (q f) -> p q f", q=4),
                rearr(Zd[0:S, :], "(q r) f -> q r f", q=4)
                [:, tto * 128:(tto + 1) * 128, :]
                .rearrange("q p f -> p q f"),
                waits=(g6["zst"],), stream="gz")
    tx = mk.dma("gp", rearr(g6["xg"].ap(), "p (q f) -> p q f", q=4),
                rearr(X1[0:S, :], "(q r) f -> q r f", q=4)
                [:, tto * 128:(tto + 1) * 128, :]
                .rearrange("q p f -> p q f"),
                waits=(g6["xst"],), stream="gx")
    g6["ld"][tto] = (tz, tx)


def p6_select(B, c, g6, tto):
    mk = B.mk
    ZOWN, XO = c["ZOWN"], c["XO"]
    tz, tx = g6["ld"].pop(tto)
    zg, xg, zacc, xacc = g6["zg"], g6["xg"], g6["zacc"], g6["xacc"]
    v = mk.op("ve", "tensor_scalar", zacc[:, :], zg[:, 0:2048],
              c["oselc"][:, 0:1], None, ALU.mult, waits=(tz,), inc=True)
    for q in range(1, 4):
        mk.drain("ve")
        v = mk.op("ve", "scalar_tensor_tensor", zacc[:, :],
                  zg[:, q * 2048:(q + 1) * 2048], c["oselc"][:, q:q + 1],
                  zacc[:, :], ALU.mult, ALU.add, inc=True)
    g6["zst"] = mk.dma("gp", ZOWN[tto * 128:(tto + 1) * 128, :],
                       zacc[:, :], waits=(v,), stream="szo")
    v = mk.op("ve", "tensor_scalar", xacc[:, :], xg[:, 0:2048],
              c["oselc"][:, 0:1], None, ALU.mult, waits=(tx,), inc=True)
    for q in range(1, 4):
        mk.drain("ve")
        v = mk.op("ve", "scalar_tensor_tensor", xacc[:, :],
                  xg[:, q * 2048:(q + 1) * 2048], c["oselc"][:, q:q + 1],
                  xacc[:, :], ALU.mult, ALU.add, inc=True)
    g6["xst"] = mk.dma("gp", XO[tto * 128:(tto + 1) * 128, :],
                       xacc[:, :], waits=(v,), stream="sxo")


def p5_mlakv(B, c):
    nc, mk = B.nc, B.mk
    inp = c["inp"]
    Zd, KVD, K2, V2, ROPET = c["Zd"], c["KVD"], c["K2"], c["V2"], c["ROPET"]
    with ExitStack() as st:
        sb = lambda n, sh, dt: B.sb(st, n, sh, dt)
        g6 = {"zg": sb("zg", [128, 4 * 2048], BF16),
              "xg": sb("xg", [128, 4 * 2048], F32),
              "zacc": sb("zacc", [128, 2048], BF16),
              "xacc": sb("xacc", [128, 2048], F32),
              "zst": None, "xst": None, "ld": {}}
        kvdw_s = sb("kvdw_s", [128, 16 * 512], BF16)
        ropekw_s = sb("ropekw_s", [128, 16 * 1024], BF16)
        kuw_s = sb("kuw_s", [128, 4 * 1024], BF16)
        vuw_s = sb("vuw_s", [128, 4 * 2048], BF16)
        zts = [sb(f"zts{i}", [128, 16 * 512], BF16) for i in range(2)]
        kvdT = sb("kvdT", [128, 4 * 512], BF16)
        kst = [sb(f"kst{i}", [128, 2048], BF16) for i in range(2)]
        v2st = [sb(f"v2st{i}", [128, V129], BF16) for i in range(2)]
        crm = sb("crm", [128, 512], F32)
        r1m = sb("r1m", [128, 512], F32)
        r2m = sb("r2m", [128, 512], F32)
        rtabm = [sb(f"rtabm{i}", [128, 1024], BF16) for i in range(2)]
        nk2t = sb("nk2t", [128, 16], F32)
        nred = sb("nred", [128, 8], F32)
        pp = [st.enter_context(nc.psum_tensor(f"pp5{i}", [128, 512], F32))
              for i in range(2)]

        wt1 = mk.dma("gp", rearr(kvdw_s.ap(), "p (ec f) -> p ec f", ec=16),
                     rearr(inp["kvdw"], "(ec p) f -> p ec f", p=128), stream="w5")
        wt2 = mk.dma("gp", rearr(ropekw_s.ap(), "p (ec f) -> p ec f", ec=16),
                     rearr(inp["ropekw"], "(ec p) f -> p ec f", p=128), stream="w5")
        wt3 = mk.dma("gp", rearr(kuw_s.ap(), "p (lc f) -> p lc f", lc=4),
                     rearr(inp["kuw"], "(lc p) f -> p lc f", p=128), stream="w5")
        wt4 = mk.dma("gp", rearr(vuw_s.ap(), "p (lc f) -> p lc f", lc=4),
                     rearr(inp["vuw"], "(lc p) f -> p lc f", p=128), stream="w5")
        z0 = mk.op("ve", "memset", c["nk2g"][:, :], 0.0, inc=True)
        vz = [mk.op("ve", "memset", v2st[i][:, :], 1.0, inc=True)
              for i in range(2)]
        zt_used = {0: None, 1: None}
        kv_used = {0: None, 1: None}
        kst_tok = {0: vz[0], 1: vz[1]}
        v2_tok = {0: vz[0], 1: vz[1]}
        rt_used = {0: None, 1: None}
        ev_done = {0: None, 1: None}
        nk_tok = z0
        jfree = None
        kvdT_used = None
        for tc in range(8):
            zsl = tc % 2
            ztok = []
            for ec in range(16):
                ztok.append(mk.dmat(
                    zts[zsl][:, ec * 512:(ec + 1) * 512],
                    Zd[tc * 512:(tc + 1) * 512, ec * 128:(ec + 1) * 128],
                    waits=(zt_used[zsl],) if ec == 0 else (), stream="zts"))
            gv = None
            for ec in range(16):
                gv = mk.op("ve", "tensor_scalar",
                           zts[zsl][:, ec * 512:(ec + 1) * 512],
                           zts[zsl][:, ec * 512:(ec + 1) * 512],
                           c["ln1c"][:, ec:ec + 1], None, ALU.mult,
                           waits=(ztok[ec],), inc=(ec == 15))
            ev4 = []
            for lc in range(4):
                ps = lc % 2
                mmt = None
                for ec in range(16):
                    mmt = mk.op("te", "matmul", pp[ps][:, :],
                                kvdw_s.ap()[:, ec * 512 + lc * 128:
                                            ec * 512 + (lc + 1) * 128],
                                zts[zsl][:, ec * 512:(ec + 1) * 512],
                                start=(ec == 0), stop=(ec == 15),
                                waits=(gv, ev_done[ps], wt1, kvdT_used)
                                if ec == 0 else (), inc=(ec == 15))
                ev = mk.op("ve", "tensor_copy", kvdT[:, lc * 512:(lc + 1) * 512],
                           pp[ps][:, :], waits=(mmt,), inc=True)
                ev_done[ps] = ev
                ev4.append(ev)
            ktok = tuple(ev4)
            for t4 in range(4):
                tt = tc * 4 + t4
                ksl = t4 % 2
                rsl = t4 % 2
                rtok = mk.dma("sy", rtabm[rsl][:, :],
                              ROPET[tt * 128:(tt + 1) * 128, 512:1536],
                              waits=(rt_used[rsl],), stream="rtm")
                k3 = rearr(kst[ksl].ap(), "p (h x) -> p h x", h=16)
                nkw = None
                for fc in range(2):
                    ps = fc % 2
                    mmt = None
                    for lc in range(4):
                        mmt = mk.op("te", "matmul", pp[ps][:, :],
                                    kvdT[:, lc * 512 + t4 * 128:
                                         lc * 512 + (t4 + 1) * 128],
                                    kuw_s.ap()[:, lc * 1024 + fc * 512:
                                               lc * 1024 + (fc + 1) * 512],
                                    start=(lc == 0), stop=(lc == 3),
                                    waits=ktok + (ev_done[ps], wt3)
                                    if lc == 0 else (), inc=(lc == 3))
                    ev = mk.op("ve", "tensor_copy",
                               k3[:, fc * 8:(fc + 1) * 8, 0:64],
                               rearr(pp[ps].ap(), "p (h i) -> p h i", h=8),
                               waits=(mmt, kst_tok[ksl]), inc=True)
                    sq = mk.op("se", "activation", c["junkb"][:, 0:512],
                               pp[ps][:, :], AF.Square,
                               waits=(mmt, ev, jfree), inc=True)
                    ev_done[ps] = sq
                    for hh in range(8):
                        jfree = mk.op("ve", "reduce_sum",
                                      nk2t[:, fc * 8 + hh:fc * 8 + hh + 1],
                                      c["junkb"][:, hh * 64:(hh + 1) * 64],
                                      AX.X, waits=(sq,) if hh == 0 else (),
                                      inc=(hh == 7))
                for fc in range(2):
                    ps = fc % 2
                    mmt = None
                    for ec in range(16):
                        mmt = mk.op("te", "matmul", pp[ps][:, :],
                                    zts[zsl][:, ec * 512 + t4 * 128:
                                             ec * 512 + (t4 + 1) * 128],
                                    ropekw_s.ap()[:, ec * 1024 + fc * 512:
                                                  ec * 1024 + (fc + 1) * 512],
                                    start=(ec == 0), stop=(ec == 15),
                                    waits=(ev_done[ps], wt2) if ec == 0 else (),
                                    inc=(ec == 15))
                    ev = mk.op("ve", "tensor_copy", crm[:, :], pp[ps][:, :],
                               waits=(mmt,), inc=True)
                    sq = mk.op("se", "activation", c["junkb"][:, 0:512],
                               pp[ps][:, :], AF.Square,
                               waits=(mmt, ev, jfree), inc=True)
                    ev_done[ps] = sq
                    for hh in range(8):
                        mk.op("ve", "reduce_sum", nred[:, hh:hh + 1],
                              c["junkb"][:, hh * 64:(hh + 1) * 64], AX.X,
                              waits=(sq,) if hh == 0 else ())
                    mk.drain("ve")
                    nkw = mk.op("ve", "tensor_add",
                                nk2t[:, fc * 8:(fc + 1) * 8],
                                nk2t[:, fc * 8:(fc + 1) * 8], nred[:, 0:8],
                                inc=True)
                    jfree = nkw
                    cr8 = rearr(crm.ap(), "p (h i) -> p h i", h=8)
                    C8 = rearr(rtabm[rsl].ap()[:, 0:512], "p (h i) -> p h i", h=8)
                    S8 = rearr(rtabm[rsl].ap()[:, 512:1024], "p (h i) -> p h i",
                               h=8)
                    R1 = rearr(r1m.ap(), "p (h i) -> p h i", h=8)
                    R2 = rearr(r2m.ap(), "p (h i) -> p h i", h=8)
                    v = mk.op("ve", "tensor_tensor", R1[:, :, :], cr8[:, :, :],
                              C8[:, :, :], ALU.mult, waits=(ev, rtok), inc=True)
                    v = mk.op("ve", "tensor_tensor", R2[:, :, 0:32],
                              cr8[:, :, 32:64], S8[:, :, 0:32], ALU.mult,
                              inc=True)
                    v = mk.op("ve", "tensor_tensor", R2[:, :, 32:64],
                              cr8[:, :, 0:32], S8[:, :, 32:64], ALU.mult,
                              inc=True)
                    mk.drain("ve")
                    v = mk.op("ve", "tensor_add",
                              k3[:, fc * 8:(fc + 1) * 8, 64:128],
                              R1[:, :, :], R2[:, :, :], inc=True)
                kst_tok[ksl] = mk.dma(
                    "gp", K2[tt * 128:(tt + 1) * 128, :], kst[ksl][:, :],
                    waits=(v,), stream="sk2")
                mk.drain("ve")
                nk_tok = mk.op("ve", "tensor_max", c["nk2g"][:, :],
                               c["nk2g"][:, :], nk2t[:, :],
                               waits=(nkw, nk_tok), inc=True)
                v3 = rearr(v2st[ksl].ap(), "p (h x) -> p h x", h=16)
                vv = None
                for fc in range(4):
                    ps = fc % 2
                    mmt = None
                    for lc in range(4):
                        mmt = mk.op("te", "matmul", pp[ps][:, :],
                                    kvdT[:, lc * 512 + t4 * 128:
                                         lc * 512 + (t4 + 1) * 128],
                                    vuw_s.ap()[:, lc * 2048 + fc * 512:
                                               lc * 2048 + (fc + 1) * 512],
                                    start=(lc == 0), stop=(lc == 3),
                                    waits=(ev_done[ps], wt4, v2_tok[ksl])
                                    if lc == 0 else (), inc=(lc == 3))
                    vv = mk.op("ve", "tensor_copy",
                               v3[:, fc * 4:(fc + 1) * 4, 0:128],
                               rearr(pp[ps].ap(), "p (h x) -> p h x", h=4),
                               waits=(mmt,), inc=True)
                    ev_done[ps] = vv
                v2_tok[ksl] = mk.dma(
                    "gp", V2[tt * 128:(tt + 1) * 128, :], v2st[ksl][:, :],
                    waits=(vv,), stream="sv2")
                rt_used[rsl] = v
            zt_used[zsl] = mmt
            kvdT_used = mmt
            if tc >= 1:
                p6_select(B, c, g6, tc - 1)
            p6_load(B, c, g6, tc)
        p6_select(B, c, g6, 7)
        mk.barrier()
        mk.flush_block()


def p6_gather(B, c):
    nc, mk = B.nc, B.mk
    Zd, X1, ZOWN, XO = c["Zd"], c["X1"], c["ZOWN"], c["XO"]
    with ExitStack() as st:
        sb = lambda n, sh, dt: B.sb(st, n, sh, dt)
        zg = sb("zg", [128, 4 * 2048], BF16)
        xg = sb("xg", [128, 4 * 2048], F32)
        zacc = sb("zacc", [128, 2048], BF16)
        xacc = sb("xacc", [128, 2048], F32)
        zst_tok = None
        xst_tok = None
        for tto in range(NTO):
            tz = mk.dma("sy", rearr(zg.ap(), "p (q f) -> p q f", q=4),
                        rearr(Zd[0:S, :], "(q r) f -> q r f", q=4)
                        [:, tto * 128:(tto + 1) * 128, :]
                        .rearrange("q p f -> p q f"),
                        waits=(zst_tok,), stream="gz")
            tx = mk.dma("sy", rearr(xg.ap(), "p (q f) -> p q f", q=4),
                        rearr(X1[0:S, :], "(q r) f -> q r f", q=4)
                        [:, tto * 128:(tto + 1) * 128, :]
                        .rearrange("q p f -> p q f"),
                        waits=(xst_tok,), stream="gx")
            v = mk.op("ve", "tensor_scalar", zacc[:, :], zg[:, 0:2048],
                      c["oselc"][:, 0:1], None, ALU.mult, waits=(tz,), inc=True)
            for q in range(1, 4):
                mk.drain("ve")
                v = mk.op("ve", "scalar_tensor_tensor", zacc[:, :],
                          zg[:, q * 2048:(q + 1) * 2048], c["oselc"][:, q:q + 1],
                          zacc[:, :], ALU.mult, ALU.add, inc=True)
            zst_tok = mk.dma("gp", ZOWN[tto * 128:(tto + 1) * 128, :],
                             zacc[:, :], waits=(v,), stream="szo")
            v = mk.op("ve", "tensor_scalar", xacc[:, :], xg[:, 0:2048],
                      c["oselc"][:, 0:1], None, ALU.mult, waits=(tx,), inc=True)
            for q in range(1, 4):
                mk.drain("ve")
                v = mk.op("ve", "scalar_tensor_tensor", xacc[:, :],
                          xg[:, q * 2048:(q + 1) * 2048], c["oselc"][:, q:q + 1],
                          xacc[:, :], ALU.mult, ALU.add, inc=True)
            xst_tok = mk.dma("gp", XO[tto * 128:(tto + 1) * 128, :],
                             xacc[:, :], waits=(v,), stream="sxo")
        mk.barrier()
        mk.flush_block()


def p7_qside(B, c):
    nc, mk = B.nc, B.mk
    inp = c["inp"]
    ZOWN, QD, Q2, ROPET = c["ZOWN"], c["QD"], c["Q2"], c["ROPET"]
    with ExitStack() as st:
        sb = lambda n, sh, dt: B.sb(st, n, sh, dt)
        qdw_s = sb("qdw_s", [128, 16 * 512], BF16)
        quw_s = sb("quw_s", [128, 4 * 1024], BF16)
        ropeqw_s = sb("ropeqw_s", [128, 4 * 1024], BF16)
        zoT = sb("zoT", [128, 16 * 512], BF16)
        qdT = sb("qdT", [128, 4 * 512], BF16)
        q2st = [sb(f"q2st{i}", [128, 2048], BF16) for i in range(2)]
        crm = sb("crmq", [128, 512], F32)
        r1m = sb("r1mq", [128, 512], F32)
        r2m = sb("r2mq", [128, 512], F32)
        rtq4 = sb("rtq4", [128, 4 * 1024], BF16)
        rtabq = sb("rtabq", [128, 1024], BF16)
        nq2t = sb("nq2t", [128, 16], F32)
        nred = sb("nredq", [128, 8], F32)
        nq2a = sb("nq2a", [128, 16], F32)
        pp = [st.enter_context(nc.psum_tensor(f"pp7{i}", [128, 512], F32))
              for i in range(2)]

        wt1 = mk.dma("gp", rearr(qdw_s.ap(), "p (ec f) -> p ec f", ec=16),
                     rearr(inp["qdw"], "(ec p) f -> p ec f", p=128), stream="w7")
        wt3 = mk.dma("gp", rearr(quw_s.ap(), "p (lc f) -> p lc f", lc=4),
                     rearr(inp["quw"], "(lc p) f -> p lc f", p=128), stream="w7")
        wt2 = mk.dma("gp", rearr(ropeqw_s.ap(), "p (lc f) -> p lc f", lc=4),
                     rearr(inp["ropeqw"], "(lc p) f -> p lc f", p=128),
                     stream="w7")
        z0 = mk.op("ve", "memset", nq2a[:, :], 0.0, inc=True)
        jfree = None
        qdT_used = None
        ev_done = {0: None, 1: None}
        kv_used = {0: None, 1: None}
        q2_tok = {0: None, 1: None}
        nk_tok = z0
        zt_used = None
        rt_used = None
        for tc in range(2):
            ztok = []
            for ec in range(16):
                ztok.append(mk.dmat(
                    zoT[:, ec * 512:(ec + 1) * 512],
                    ZOWN[tc * 512:(tc + 1) * 512, ec * 128:(ec + 1) * 128],
                    waits=(zt_used,) if ec == 0 else (), stream="zoT"))
            gv = None
            for ec in range(16):
                gv = mk.op("ve", "tensor_scalar",
                           zoT[:, ec * 512:(ec + 1) * 512],
                           zoT[:, ec * 512:(ec + 1) * 512],
                           c["ln1c"][:, ec:ec + 1], None, ALU.mult,
                           waits=(ztok[ec],), inc=(ec == 15))
            ev4 = []
            for lc in range(4):
                ps = lc % 2
                mmt = None
                for ec in range(16):
                    mmt = mk.op("te", "matmul", pp[ps][:, :],
                                qdw_s.ap()[:, ec * 512 + lc * 128:
                                           ec * 512 + (lc + 1) * 128],
                                zoT[:, ec * 512:(ec + 1) * 512],
                                start=(ec == 0), stop=(ec == 15),
                                waits=(gv, ev_done[ps], wt1, qdT_used)
                                if ec == 0 else (), inc=(ec == 15))
                ev = mk.op("ve", "tensor_copy", qdT[:, lc * 512:(lc + 1) * 512],
                           pp[ps][:, :], waits=(mmt,), inc=True)
                ev_done[ps] = ev
                ev4.append(ev)
            ktok = tuple(ev4)
            for t4 in range(4):
                tt = tc * 4 + t4
                ksl = t4 % 2
                trt = mk.dma("sy", rearr(rtq4.ap(), "p (q w) -> p q w", q=4),
                             rearr(ROPET[0:S, 512:1536], "(q r) w -> q r w", q=4)
                             [:, tt * 128:(tt + 1) * 128, :]
                             .rearrange("q p w -> p q w"),
                             waits=(rt_used,), stream="rtq")
                v = mk.op("ve", "tensor_scalar", rtabq[:, :], rtq4[:, 0:1024],
                          c["oselc"][:, 0:1], None, ALU.mult, waits=(trt,),
                          inc=True)
                for q in range(1, 4):
                    mk.drain("ve")
                    v = mk.op("ve", "scalar_tensor_tensor", rtabq[:, :],
                              rtq4[:, q * 1024:(q + 1) * 1024],
                              c["oselc"][:, q:q + 1], rtabq[:, :],
                              ALU.mult, ALU.add, inc=True)
                rtok = v
                k3 = rearr(q2st[ksl].ap(), "p (h x) -> p h x", h=16)
                for fc in range(2):
                    ps = fc % 2
                    mmt = None
                    for lc in range(4):
                        mmt = mk.op("te", "matmul", pp[ps][:, :],
                                    qdT[:, lc * 512 + t4 * 128:
                                        lc * 512 + (t4 + 1) * 128],
                                    quw_s.ap()[:, lc * 1024 + fc * 512:
                                               lc * 1024 + (fc + 1) * 512],
                                    start=(lc == 0), stop=(lc == 3),
                                    waits=ktok + (ev_done[ps], wt3,
                                                  q2_tok[ksl])
                                    if lc == 0 else (), inc=(lc == 3))
                    ev = mk.op("ve", "tensor_copy",
                               k3[:, fc * 8:(fc + 1) * 8, 0:64],
                               rearr(pp[ps].ap(), "p (h i) -> p h i", h=8),
                               waits=(mmt,), inc=True)
                    sq = mk.op("se", "activation", c["junkb"][:, 0:512],
                               pp[ps][:, :], AF.Square,
                               waits=(mmt, ev, jfree), inc=True)
                    ev_done[ps] = sq
                    for hh in range(8):
                        jfree = mk.op("ve", "reduce_sum",
                                      nq2t[:, fc * 8 + hh:fc * 8 + hh + 1],
                                      c["junkb"][:, hh * 64:(hh + 1) * 64],
                                      AX.X, waits=(sq,) if hh == 0 else (),
                                      inc=(hh == 7))
                nkw = None
                for fc in range(2):
                    ps = fc % 2
                    mmt = None
                    for lc in range(4):
                        mmt = mk.op("te", "matmul", pp[ps][:, :],
                                    qdT[:, lc * 512 + t4 * 128:
                                        lc * 512 + (t4 + 1) * 128],
                                    ropeqw_s.ap()[:, lc * 1024 + fc * 512:
                                                  lc * 1024 + (fc + 1) * 512],
                                    start=(lc == 0), stop=(lc == 3),
                                    waits=(ev_done[ps], wt2) if lc == 0 else (),
                                    inc=(lc == 3))
                    ev = mk.op("ve", "tensor_copy", crm[:, :], pp[ps][:, :],
                               waits=(mmt,), inc=True)
                    sq = mk.op("se", "activation", c["junkb"][:, 0:512],
                               pp[ps][:, :], AF.Square,
                               waits=(mmt, ev, jfree), inc=True)
                    ev_done[ps] = sq
                    for hh in range(8):
                        mk.op("ve", "reduce_sum", nred[:, hh:hh + 1],
                              c["junkb"][:, hh * 64:(hh + 1) * 64], AX.X,
                              waits=(sq,) if hh == 0 else ())
                    mk.drain("ve")
                    nkw = mk.op("ve", "tensor_add",
                                nq2t[:, fc * 8:(fc + 1) * 8],
                                nq2t[:, fc * 8:(fc + 1) * 8], nred[:, 0:8],
                                inc=True)
                    jfree = nkw
                    cr8 = rearr(crm.ap(), "p (h i) -> p h i", h=8)
                    C8 = rearr(rtabq.ap()[:, 0:512], "p (h i) -> p h i", h=8)
                    S8 = rearr(rtabq.ap()[:, 512:1024], "p (h i) -> p h i", h=8)
                    R1 = rearr(r1m.ap(), "p (h i) -> p h i", h=8)
                    R2 = rearr(r2m.ap(), "p (h i) -> p h i", h=8)
                    v = mk.op("ve", "tensor_tensor", R1[:, :, :], cr8[:, :, :],
                              C8[:, :, :], ALU.mult, waits=(ev, rtok), inc=True)
                    v = mk.op("ve", "tensor_tensor", R2[:, :, 0:32],
                              cr8[:, :, 32:64], S8[:, :, 0:32], ALU.mult,
                              inc=True)
                    v = mk.op("ve", "tensor_tensor", R2[:, :, 32:64],
                              cr8[:, :, 0:32], S8[:, :, 32:64], ALU.mult,
                              inc=True)
                    mk.drain("ve")
                    v = mk.op("ve", "tensor_add",
                              k3[:, fc * 8:(fc + 1) * 8, 64:128],
                              R1[:, :, :], R2[:, :, :], inc=True)
                q2_tok[ksl] = mk.dma(
                    "sy", Q2[tt * 128:(tt + 1) * 128, :], q2st[ksl][:, :],
                    waits=(v,), stream="sq2")
                rt_used = v
                qdT_used = mmt
                mk.drain("ve")
                nk_tok = mk.op("ve", "tensor_max", nq2a[:, :], nq2a[:, :],
                               nq2t[:, :], waits=(nkw, nk_tok), inc=True)
            zt_used = mmt
        finalize_bound(B, c, st, nq2a, c["nk2g"], nk_tok, nk_tok,
                       c["bias2"], rowbase=32)
        mk.barrier()
        mk.flush_block()


def p9_ow(B, c):
    nc, mk = B.nc, B.mk
    inp = c["inp"]
    O2d, XO, Z2, SP = c["O2d"], c["XO"], c["Z2"], c["SP"]
    with ExitStack() as st:
        sb = lambda n, sh, dt: B.sb(st, n, sh, dt)
        ow_s = sb("ow_s", [128, 16 * 2048], BF16)
        o2t = [sb(f"o2t{i}", [128, 16 * 512], BF16) for i in range(2)]
        xo = [sb(f"xo{i}", [128, 2048], F32) for i in range(2)]
        hs = [sb(f"hs{i}", [128, 2048], F32) for i in range(2)]
        z2st = [sb(f"z2st{i}", [128, 2048], BF16) for i in range(2)]
        spst = [sb(f"spst{i}", [128, 2048], F32) for i in range(2)]
        cols = [sb(f"cols9{i}", [128, 8], F32) for i in range(2)]
        pp = [st.enter_context(nc.psum_tensor(f"pp9{i}", [128, 512], F32))
              for i in range(2)]
        wtok = []
        for piece in range(4):
            sl = slice(piece * 4, (piece + 1) * 4)
            wtok.append(mk.dma(
                "gp", rearr(ow_s.ap(), "p (dc f) -> p dc f", dc=16)[:, sl, :],
                rearr(inp["ow"], "(dc p) f -> p dc f", p=128)[:, sl, :],
                stream="wo9"))
        o2_used = {0: None, 1: None}
        ev_done = {0: None, 1: None}
        buf_used = {0: (), 1: ()}
        ot_grp = {}
        for tto in range(NTO):
            sl_ = tto % 2
            if tto % 4 == 0:
                gsl = (tto // 4) % 2
                ot_grp[tto // 4] = [mk.dmat(
                    o2t[gsl][:, dc * 512:(dc + 1) * 512],
                    O2d[tto * 128:(tto + 4) * 128, dc * 128:(dc + 1) * 128],
                    waits=(o2_used[gsl],) if dc == 0 else (),
                    stream="o2t") for dc in range(16)]
            ot = ot_grp[tto // 4]
            xt = mk.dma("sy", xo[sl_][:, :], XO[tto * 128:(tto + 1) * 128, :],
                        waits=buf_used[sl_] or (), stream="xo9")
            evl = None
            mmt = None
            for fc in range(4):
                ps = fc % 2
                for dc in range(16):
                    mmt = mk.op(
                        "te", "matmul", pp[ps][:, :],
                        o2t[(tto // 4) % 2][:, dc * 512 + (tto % 4) * 128:
                                            dc * 512 + (tto % 4 + 1) * 128],
                        ow_s.ap()[:, dc * 2048 + fc * 512:
                                  dc * 2048 + (fc + 1) * 512],
                        start=(dc == 0), stop=(dc == 15),
                        waits=tuple(ot) + tuple(wtok) + (ev_done[ps],)
                        if dc == 0 else (), inc=(dc == 15))
                evl = mk.op("ve", "scalar_tensor_tensor",
                            hs[sl_][:, fc * 512:(fc + 1) * 512],
                            pp[ps][:, :], 0.0, xo[sl_][:, fc * 512:(fc + 1) * 512],
                            ALU.bypass, ALU.add, waits=(mmt, xt), inc=True)
                ev_done[ps] = evl
            if tto % 4 == 3:
                o2_used[(tto // 4) % 2] = mmt
            cl = cols[sl_]
            sq = mk.op("se", "activation", c["junkb"][:, :], hs[sl_][:, :],
                       AF.Square, accum_out=cl[:, 0:1], waits=(evl,), inc=True)
            v = mk.op("ve", "tensor_single_scalar", cl[:, 1:2], cl[:, 0:1],
                      1.0 / D, ALU.mult, waits=(sq,), inc=True)
            sq2 = mk.op("se", "activation", cl[:, 2:3], cl[:, 1:2], AF.Sqrt,
                        bias=c["eps_col"][:, :], waits=(v,), inc=True)
            v = mk.op("ve", "reciprocal", cl[:, 3:4], cl[:, 2:3],
                      waits=(sq2,), inc=True)
            mk.drain("ve")
            v = mk.op("ve", "tensor_scalar", z2st[sl_][:, :], hs[sl_][:, :],
                      cl[:, 3:4], None, ALU.mult, inc=True)
            tz = mk.dma("gp", Z2[tto * 128:(tto + 1) * 128, :], z2st[sl_][:, :],
                        waits=(v,), stream="sz2")
            v2 = mk.op("ve", "tensor_add", spst[sl_][:, :], hs[sl_][:, :],
                       xo[sl_][:, :], waits=(v,), inc=True)
            ts_ = mk.dma("gp", SP[tto * 128:(tto + 1) * 128, :], spst[sl_][:, :],
                         waits=(v2,), stream="ssp")
            buf_used[sl_] = (tz, ts_)
        mk.barrier()
        mk.flush_block()


def p10_moe(B, c):
    nc, mk = B.nc, B.mk
    inp = c["inp"]
    Z2, SP, out = c["Z2"], c["SP"], c["out"]
    with ExitStack() as st:
        sb = lambda n, sh, dt: B.sb(st, n, sh, dt)
        rw_s = sb("rw_s", [128, 16 * 8], BF16)
        rb_s = sb("rb_s", [1, NRE], BF16)
        z2T = sb("z2T", [128, 16 * 512], BF16)
        wg_s = [sb(f"wg_s{i}", [128, 16 * 512], BF16) for i in range(2)]
        wu_s = [sb(f"wu_s{i}", [128, 16 * 512], BF16) for i in range(2)]
        wd_s = [sb(f"wd_s{i}", [128, 4 * 2048], BF16) for i in range(2)]
        gbuf = [sb(f"gbuf{i}", [128, 8 * 512], BF16) for i in range(2)]
        sg = [sb(f"sgb{i}", [128, 512], BF16) for i in range(2)]
        su = [sb(f"sub{i}", [128, 512], BF16) for i in range(2)]
        wsb = sb("wsb", [128, 512], F32)
        wT = sb("wT", [8, 1024], BF16)
        wrow = sb("wrow", [1, 512], BF16)
        probs = sb("probs", [128, 8], F32)
        rcols = sb("rcols", [128, 8], F32)
        eqs = sb("eqs", [128, 8], F32)
        p2s = sb("p2s", [128, 8], F32)
        wts = sb("wts", [128, 8], F32)
        wtb = sb("wtb", [128, 8], BF16)
        moeacc = sb("moeacc", [128, 4 * 2048], F32)
        fin = [sb(f"fin{i}", [128, 2048], F32) for i in range(2)]
        cols = [sb(f"cols10{i}", [128, 8], F32) for i in range(2)]

        t_rw = mk.dma("gp", rearr(rw_s.ap(), "p (ec f) -> p ec f", ec=16)
                      [:, :, 0:NRE],
                      rearr(inp["rw"], "(ec p) f -> p ec f", p=128),
                      stream="w10")
        t_rb = mk.dma("gp", rb_s[:, :], inp["rbias"], stream="w10")
        out_toks = []
        zt_used = None
        wg_used = {0: None, 1: None}
        wd_used = {0: None, 1: None}
        for tc in range(2):
            ztok = []
            for ec in range(16):
                ztok.append(mk.dmat(
                    z2T[:, ec * 512:(ec + 1) * 512],
                    Z2[tc * 512:(tc + 1) * 512, ec * 128:(ec + 1) * 128],
                    waits=(zt_used,) if ec == 0 else (), stream="z2T"))
            gv = None
            for ec in range(16):
                gv = mk.op("ve", "tensor_scalar",
                           z2T[:, ec * 512:(ec + 1) * 512],
                           z2T[:, ec * 512:(ec + 1) * 512],
                           c["ln2c"][:, ec:ec + 1], None, ALU.mult,
                           waits=(ztok[ec],), inc=(ec == 15))
            with ExitStack() as st2:
                pr = st2.enter_context(nc.psum_tensor(f"pr{tc}", [128, 8], F32))
                ptr = st2.enter_context(nc.psum_tensor(f"ptr{tc}", [8, 128], BF16))
                wt_tok = None
                for t4 in range(4):
                    mmt = None
                    for ec in range(16):
                        mmt = mk.op("te", "matmul", pr[:, 0:NRE],
                                    z2T[:, ec * 512 + t4 * 128:
                                        ec * 512 + (t4 + 1) * 128],
                                    rw_s.ap()[:, ec * 8:ec * 8 + NRE],
                                    start=(ec == 0), stop=False,
                                    waits=(gv, t_rw, t_rb, wt_tok)
                                    if ec == 0 else ())
                    mmt = mk.op("te", "matmul", pr[:, 0:NRE],
                                c["ones_rowb"][:, :], rb_s[:, :],
                                start=False, stop=True, inc=True)
                    sg_ = mk.op("se", "activation", probs[:, 0:NRE],
                                pr[:, 0:NRE], AF.Sigmoid, waits=(mmt,),
                                inc=True)
                    v = mk.op("ve", "reduce_max", rcols[:, 0:1], probs[:, 0:NRE],
                              AX.X, waits=(sg_,), inc=True)
                    mk.drain("ve")
                    v = mk.op("ve", "tensor_scalar", eqs[:, 0:NRE],
                              probs[:, 0:NRE], rcols[:, 0:1], None,
                              ALU.is_equal, inc=True)
                    mk.drain("ve")
                    v = mk.op("ve", "scalar_tensor_tensor", p2s[:, 0:NRE],
                              eqs[:, 0:NRE], -1e30, probs[:, 0:NRE],
                              ALU.mult, ALU.add, inc=True)
                    mk.drain("ve")
                    v = mk.op("ve", "reduce_max", rcols[:, 1:2], p2s[:, 0:NRE],
                              AX.X, inc=True)
                    mk.drain("ve")
                    v = mk.op("ve", "tensor_scalar", p2s[:, 0:NRE],
                              p2s[:, 0:NRE], rcols[:, 1:2], None,
                              ALU.is_equal, inc=True)
                    mk.drain("ve")
                    v = mk.op("ve", "tensor_add", eqs[:, 0:NRE], eqs[:, 0:NRE],
                              p2s[:, 0:NRE], inc=True)
                    v = mk.op("ve", "tensor_add", rcols[:, 2:3], rcols[:, 0:1],
                              rcols[:, 1:2], inc=True)
                    mk.drain("ve")
                    v = mk.op("ve", "tensor_single_scalar", rcols[:, 2:3],
                              rcols[:, 2:3], 8.0, ALU.mult, inc=True)
                    mk.drain("ve")
                    v = mk.op("ve", "reciprocal", rcols[:, 3:4], rcols[:, 2:3],
                              inc=True)
                    v = mk.op("ve", "tensor_mul", wts[:, 0:NRE], probs[:, 0:NRE],
                              eqs[:, 0:NRE], inc=True)
                    mk.drain("ve")
                    v = mk.op("ve", "tensor_scalar", wts[:, 0:NRE], wts[:, 0:NRE],
                              rcols[:, 3:4], None, ALU.mult, inc=True)
                    v = mk.op("ve", "memset", wts[:, NRE:8], 0.125, inc=True)
                    mk.drain("ve")
                    v = mk.op("ve", "tensor_copy", wtb[:, :], wts[:, :], inc=True)
                    tr = mk.op("te", "transpose", ptr[:, :], wtb[:, :],
                               c["ident"][:, :], waits=(v,), inc=True)
                    wt_tok = mk.op("ve", "tensor_copy",
                                   wT[0:8, tc * 512 + t4 * 128:
                                      tc * 512 + (t4 + 1) * 128],
                                   ptr[0:8, :], waits=(tr,), inc=True)
            t_wtd = mk.dma("sy", c["WTD"][0:8, tc * 512:(tc + 1) * 512],
                           wT[0:8, tc * 512:(tc + 1) * 512], waits=(wt_tok,),
                           stream="wtd")
            wv = None
            stp2 = ExitStack()
            pwb = stp2.enter_context(nc.psum_tensor(f"pwb{tc}", [128, 512], F32))
            pg = [stp2.enter_context(nc.psum_tensor(f"pg{tc}_{i}", [128, 512], F32))
                  for i in range(4)]
            pd = [stp2.enter_context(nc.psum_tensor(f"pd{tc}_{i}", [128, 512], F32))
                  for i in range(2)]
            ma_tok = {}
            wrow_used = None
            wsb_used = None
            uv_last = None
            gb_used = None
            def gate_up(e):
                nonlocal wv, uv_last, wrow_used, wsb_used
                esl = e % 2
                gsl = e % 2
                wg_in = c["WGB"][e]
                wu_in = c["WUB"][e]
                gb_prev = ma_tok.get(("gfree", gsl))
                for ih in range(2):
                    tg = mk.dma("gp", rearr(wg_s[esl].ap(),
                                            "p (ec i) -> p ec i", ec=16),
                                rearr(wg_in[:, ih * 512:(ih + 1) * 512],
                                      "(ec p) i -> p ec i", p=128),
                                waits=(wg_used[esl], c["wcm_tok"]),
                                stream="wgu")
                    tu = mk.dma("gp", rearr(wu_s[esl].ap(),
                                            "p (ec i) -> p ec i", ec=16),
                                rearr(wu_in[:, ih * 512:(ih + 1) * 512],
                                      "(ec p) i -> p ec i", p=128),
                                waits=(wg_used[esl], c["wcm_tok"]),
                                stream="wgu")
                    if ih == 0:
                        twr = mk.dma("sy", wrow[:, :],
                                     c["WTD"][e:e + 1, tc * 512:(tc + 1) * 512],
                                     waits=(t_wtd, wrow_used), stream="wrw")
                        bt = mk.op("te", "matmul", pwb[:, :],
                                   c["ones_rowb"][:, :], wrow[:, :],
                                   start=True, stop=True,
                                   waits=(twr, wsb_used), inc=True)
                        wv = mk.op("ve", "tensor_copy", wsb[:, :], pwb[:, :],
                                   waits=(bt, uv_last), inc=True)
                        wrow_used = wv
                        wsb_used = wv
                    for ic4 in range(4):
                        ic = ih * 4 + ic4
                        pslot = ic4 % 2
                        mg = None
                        for ec in range(16):
                            mg = mk.op("te", "matmul", pg[pslot][:, :],
                                       wg_s[esl].ap()[:, ec * 512 + ic4 * 128:
                                                      ec * 512 + (ic4 + 1) * 128],
                                       z2T[:, ec * 512:(ec + 1) * 512],
                                       start=(ec == 0), stop=(ec == 15),
                                       waits=(tg, gv, wt_tok,
                                              ma_tok.get(("sg", pslot)))
                                       if ec == 0 else (), inc=(ec == 15))
                        sil = mk.op("se", "activation", sg[pslot][:, :],
                                    pg[pslot][:, :],
                                    AF.Silu, waits=(mg, ma_tok.get(("gbd", pslot))),
                                    inc=True)
                        ma_tok[("sg", pslot)] = sil
                        mu = None
                        for ec in range(16):
                            mu = mk.op("te", "matmul", pg[2 + pslot][:, :],
                                       wu_s[esl].ap()[:, ec * 512 + ic4 * 128:
                                                      ec * 512 + (ic4 + 1) * 128],
                                       z2T[:, ec * 512:(ec + 1) * 512],
                                       start=(ec == 0), stop=(ec == 15),
                                       waits=(tu, ma_tok.get(("su", pslot)))
                                       if ec == 0 else (), inc=(ec == 15))
                        uv = mk.op("ve", "tensor_tensor", su[pslot][:, :],
                                   pg[2 + pslot][:, :],
                                   wsb[:, :], ALU.mult, waits=(mu, wv),
                                   inc=True)
                        ma_tok[("su", pslot)] = uv
                        uv_last = uv
                        gb = mk.op("ve", "tensor_tensor",
                                   gbuf[gsl][:, ic * 512:(ic + 1) * 512],
                                   sg[pslot][:, :], su[pslot][:, :], ALU.mult,
                                   waits=(sil, gb_prev, uv), inc=True)
                        ma_tok[("sg", pslot)] = gb
                        ma_tok[("gbd", pslot)] = gb
                        ma_tok[("gb", gsl, ic)] = gb
                    wg_used[esl] = mu

            def down(e):
                nonlocal dn_last
                gsl = e % 2
                wd_in = c["WDB"][e]
                tw0 = mk.dma("gp", rearr(wd_s[0].ap(), "p (ic f) -> p ic f",
                                         ic=4),
                             rearr(wd_in[0:512, :], "(ic p) f -> p ic f", p=128),
                             waits=(wd_used[0], c["wcm_tok"]), stream="wd10")
                tw1 = mk.dma("gp", rearr(wd_s[1].ap(), "p (ic f) -> p ic f",
                                         ic=4),
                             rearr(wd_in[512:1024, :], "(ic p) f -> p ic f",
                                   p=128),
                             waits=(wd_used[1], c["wcm_tok"]), stream="wd10")
                md = None
                for t4 in range(4):
                    for fc in range(4):
                        ps = (t4 * 4 + fc) % 2
                        for ic in range(8):
                            md = mk.op(
                                "te", "matmul", pd[ps][:, :],
                                gbuf[gsl][:, ic * 512 + t4 * 128:
                                          ic * 512 + (t4 + 1) * 128],
                                wd_s[ic // 4].ap()[:, (ic % 4) * 2048 + fc * 512:
                                                   (ic % 4) * 2048 + (fc + 1) * 512],
                                start=(ic == 0), stop=(ic == 7),
                                waits=((tw0, ma_tok[("gb", gsl, 7)],
                                        ma_tok.get(("pd", ps)))
                                       if ic == 0 else
                                       ((tw1,) if ic == 4 else ())),
                                inc=(ic == 7))
                        dst = moeacc.ap()[:, t4 * 2048 + fc * 512:
                                          t4 * 2048 + (fc + 1) * 512]
                        prev_acc = ma_tok.get(("macc", t4, fc))
                        if e == 0:
                            mv = mk.op("ve", "tensor_copy", dst, pd[ps][:, :],
                                       waits=(md, prev_acc), inc=True)
                        else:
                            mv = mk.op("ve", "tensor_add", dst, dst,
                                       pd[ps][:, :], waits=(md, prev_acc),
                                       inc=True)
                        ma_tok[("pd", ps)] = mv
                        ma_tok[("macc", t4, fc)] = mv
                        dn_last = mv
                wd_used[0] = md
                wd_used[1] = md
                ma_tok[("gfree", gsl)] = md

            dn_last = None
            for e in range(8):
                gate_up(e)
                if e >= 1:
                    down(e - 1)
            down(7)
            for t4 in range(4):
                sl_ = t4 % 2
                tt = tc * 4 + t4
                tsp = mk.dma("sy", fin[sl_][:, :],
                             SP[tt * 128:(tt + 1) * 128, :],
                             waits=(out_toks[-1] if out_toks else None,),
                             stream="spl")
                v = mk.op("ve", "tensor_add", fin[sl_][:, :],
                          fin[sl_][:, :],
                          moeacc.ap()[:, t4 * 2048:(t4 + 1) * 2048],
                          waits=(tsp, dn_last), inc=True)
                cl = cols[sl_]
                sq = mk.op("se", "activation", c["junkb"][:, :], fin[sl_][:, :],
                           AF.Square, accum_out=cl[:, 0:1], waits=(v,), inc=True)
                v = mk.op("ve", "reduce_sum", cl[:, 1:2], fin[sl_][:, :], AX.X,
                          waits=(v,), inc=True)
                mk.drain("ve")
                v = mk.op("ve", "tensor_single_scalar", cl[:, 1:2], cl[:, 1:2],
                          1.0 / D, ALU.mult, waits=(sq,), inc=True)
                mk.drain("ve")
                v = mk.op("ve", "tensor_tensor", cl[:, 2:3], cl[:, 1:2],
                          cl[:, 1:2], ALU.mult, inc=True)
                v = mk.op("ve", "tensor_scalar", cl[:, 3:4], cl[:, 0:1],
                          1.0 / D, None, ALU.mult, inc=True)
                mk.drain("ve")
                v = mk.op("ve", "tensor_sub", cl[:, 3:4], cl[:, 3:4],
                          cl[:, 2:3], inc=True)
                sq2 = mk.op("se", "activation", cl[:, 4:5], cl[:, 3:4], AF.Sqrt,
                            bias=c["eps_col"][:, :], waits=(v,), inc=True)
                v = mk.op("ve", "reciprocal", cl[:, 5:6], cl[:, 4:5],
                          waits=(sq2,), inc=True)
                mk.drain("ve")
                v = mk.op("ve", "tensor_tensor", cl[:, 6:7], cl[:, 1:2],
                          cl[:, 5:6], ALU.mult, inc=True)
                mk.drain("ve")
                v = mk.op("ve", "tensor_single_scalar", cl[:, 6:7], cl[:, 6:7],
                          -1.0, ALU.mult, inc=True)
                mk.drain("ve")
                v = mk.op("ve", "tensor_scalar", fin[sl_][:, :], fin[sl_][:, :],
                          cl[:, 5:6], cl[:, 6:7], ALU.mult, ALU.add, inc=True)
                out_toks.append(mk.dma("sy", out[tt * 128:(tt + 1) * 128, :],
                                       fin[sl_][:, :], waits=(v,),
                                       stream="sout"))
            zt_used = wg_used[1]
            stp2.close()
        mk.wait("sy", (mk.sld, mk.sld.n))
        mk.wait("gp", (mk.gld, mk.gld.n))
        mk.flush_block()


def make_in_maps(inputs):
    f32 = lambda a: np.ascontiguousarray(np.asarray(a, dtype=np.float32))
    x = f32(inputs["L0_attn_hidden_states"])
    base = dict(
        wqkvT=f32(np.asarray(inputs["L0_attn_w_qkv"]).T),
        woutT=f32(np.asarray(inputs["L0_attn_w_out"]).T),
        ln1=f32(inputs["L0_ffn_ln1_w"]).reshape(1, D),
        ln2=f32(inputs["L0_ffn_ln2_w"]).reshape(1, D),
        kvdw=f32(inputs["L0_ffn_kv_d_w"]), qdw=f32(inputs["L0_ffn_q_d_w"]),
        kuw=f32(inputs["L0_ffn_k_u_w"]), vuw=f32(inputs["L0_ffn_v_u_w"]),
        quw=f32(inputs["L0_ffn_q_u_w"]),
        ropekw=f32(inputs["L0_ffn_rope_k_w"]),
        ropeqw=f32(inputs["L0_ffn_rope_q_w"]),
        ow=f32(inputs["L0_ffn_o_w"]),
        shg=f32(inputs["L0_ffn_sh_gate"]), shu=f32(inputs["L0_ffn_sh_up"]),
        shd=f32(inputs["L0_ffn_sh_down"]),
        rg=f32(inputs["L0_ffn_r_gate"]), ru=f32(inputs["L0_ffn_r_up"]),
        rd=f32(inputs["L0_ffn_r_down"]),
        rw=f32(inputs["L0_ffn_router_w"]),
        rbias=f32(inputs["L0_ffn_router_b"]).reshape(1, NRE),
    )
    xb = [f32(x[b]) for b in range(2)]
    xbT = [f32(x[b].T) for b in range(2)]
    maps = []
    for cix in range(8):
        b, qo = cix // 4, cix % 4
        m = dict(base)
        m["xb"] = xb[b]
        m["xbT"] = xbT[b]
        m["qstart"] = np.array([[qo * 1024.0]], np.float32)
        osel = np.zeros((1, 4), np.float32)
        osel[0, qo] = 1.0
        m["osel"] = osel
        maps.append(m)
    return maps


def kernel(**inputs):
    nc = bass.Bass("TRN2", target_bir_lowering=False)
    build(nc)
    maps = make_in_maps(inputs)
    res = run_bass_kernel_spmd(nc, maps, core_ids=list(range(8)))
    out = np.zeros((2, S, D), np.float32)
    for cix in range(8):
        b, qo = cix // 4, cix % 4
        out[b, qo * OWN:(qo + 1) * OWN] = res.results[cix]["out"]
    return out
